# version 14
# baseline (speedup 1.0000x reference)
import numpy as np
from contextlib import ExitStack

import concourse.bass as bass
import concourse.mybir as mybir
from concourse.bass_utils import run_bass_kernel_spmd

F32 = mybir.dt.float32
BF16 = mybir.dt.bfloat16
I32 = mybir.dt.int32
AF = mybir.ActivationFunctionType
OP = mybir.AluOpType
AX = mybir.AxisListType

B, T, DM = 4, 4096, 1024
DI, DH, NS = 2048, 1024, 16
TC, NCH = 512, 8
TOK = 2048
NFC = TOK // TC
INNER = 4096
GROUPS = [[0, 1], [2, 3], [4, 5], [6, 7]]
KD = DH // 128
KM = DM // 128
KI = INNER // 128

C_ONE, C_NEGS, C_EPS, C_NEPS = 0, 1, 17, 18
NCONST = 24

DEBUG_TAPS = []


class Sched:
    ENGS = ("sync", "scalar", "vector", "tensor", "gpsimd")

    def __init__(self):
        self.ops = {e: [] for e in self.ENGS}
        self.cnt = {}

    def emit(self, eng, fn, waits=(), incs=()):
        rec = []
        for s, d in incs:
            self.cnt[s] = self.cnt.get(s, 0) + d
            rec.append((s, d))
        self.ops[eng].append((fn, list(waits), rec))
        return {s: self.cnt[s] for s, _ in rec}

    def now(self, sem):
        return self.cnt.get(sem, 0)


def build_nc(taps=()):
    taps = dict(taps)
    nc = bass.Bass("TRN2", target_bir_lowering=False, debug=False)

    def din(name, shape, dtype=F32):
        return nc.dram_tensor(name, shape, dtype, kind="ExternalInput").ap()

    def dscr(name, shape, dtype=F32):
        return nc.dram_tensor(name, shape, dtype).ap()

    xfull = din("xfull", [T, DM])
    xmy = din("xmy", [TOK, DM])
    win = din("win", [DM, DI])
    b_in = din("b_in", [DI])
    wxp = din("wxp", [DH, 2112])
    bxp_u = din("bxp_u", [DH])
    bxp_ds = din("bxp_ds", [16])
    bxp_b = din("bxp_b", [16])
    dtw = din("dtw", [NS, DH])
    dtb = din("dtb", [DH])
    convw = din("convw", [DH, 4])
    convb = din("convb", [DH])
    dvec = din("dvec", [DH])
    wout = din("wout", [DH, DM])
    bout = din("bout", [DM])
    wg = din("wg", [INNER, DM])
    wu = din("wu", [INNER, DM])
    wd = din("wd", [DM, INNER])
    consts = din("consts", [128, NCONST])
    onesbf = din("onesbf", [1, 128], BF16)
    selbf = din("selbf", [16, 16 * 128], BF16)
    onesf = din("onesf", [128, 1])
    onesfr = din("onesfr", [1, 128])
    ident = din("ident", [128, 128], BF16)

    cc1_in = [dscr(f"cc1i{c}", [2112, TC]) for c in range(NCH)]
    cc1_out = [dscr(f"cc1o{c}", [1056, TC]) for c in range(NCH)]
    cc2_in = [dscr(f"cc2i{c}", [2 * TC, DM]) for c in range(NFC)]
    cc2_out = [dscr(f"cc2o{c}", [TC, DM]) for c in range(NFC)]
    wgT_d = dscr("wgT_d", [KM * 32 * 128, 128], BF16)
    scr_w = dscr("scr_w", [128, 1])
    scr_m = dscr("scr_m", [1, 2])
    wuT_d = dscr("wuT_d", [KM * 32 * 128, 128], BF16)
    wdT_d = dscr("wdT_d", [KI * 8 * 128, 128], BF16)

    y_out = nc.dram_tensor("y", [TOK, DM], F32, kind="ExternalOutput").ap()
    tap_out = {}
    for tname, shape in taps.items():
        tap_out[tname] = nc.dram_tensor("tap_" + tname, shape, F32,
                                        kind="ExternalOutput").ap()

    S = Sched()
    E = S.emit

    outer = ExitStack()

    def sbo(name, shape, dtype=F32):
        return outer.enter_context(nc.sbuf_tensor(name, shape, dtype))

    with outer:
        cc = sbo("cc", [128, NCONST])
        ones_bf = sbo("ones_bf", [1, 128], BF16)
        sel_bf = sbo("sel_bf", [16, 16 * 128], BF16)
        ones_f = sbo("ones_f", [128, 1])
        ones_fr = sbo("ones_fr", [1, 128])
        idn = sbo("idn", [128, 128], BF16)
        binp = sbo("binp", [128, 16])
        bxpu = sbo("bxpu", [128, KD])
        bxpds = sbo("bxpds", [16, 1])
        bxpb = sbo("bxpb", [16, 1])
        dtb_t = sbo("dtb_t", [128, KD])
        cwt = sbo("cwt", [128, 4 * KD])
        cbt = sbo("cbt", [128, KD])
        dvt = sbo("dvt", [128, KD])
        boutb = sbo("boutb", [128, DM])
        boutrow = sbo("boutrow", [1, DM], BF16)
        carry = sbo("carry", [128, KD * NS], BF16)
        mclipc = sbo("mclipc", [128, 4])
        col = sbo("col", [128, 18])
        wstg = [sbo(f"wstg{i}", [128, 1056]) for i in range(2)]
        pst = ExitStack()
        p_tr = pst.enter_context(nc.psum_tensor("p_tr", [128, 512], BF16))
        p_mm = [pst.enter_context(nc.psum_tensor(f"p_mm{i}", [128, 512], F32)) for i in range(2)]
        p_bc = [pst.enter_context(nc.psum_tensor(f"p_bc{i}", [128, 512], F32)) for i in range(2)]
        p_op = [pst.enter_context(nc.psum_tensor(f"p_op{i}", [128, 512], F32)) for i in range(2)]
        outer.enter_context(pst.pop_all())

        SEMNAMES = ["cs", "cv_", "ct", "cc1", "cc2", "q_const", "q_wl0", "q_wl1",
                    "q_x", "q_s0", "q_s1", "q_s2", "q_s3", "q_u", "q_gw", "q_xt",
                    "q_h0", "q_h1", "q_wq", "q_t0", "q_t1", "q_f0", "q_y", "q_tap"]
        sems = {}
        for sname in SEMNAMES:
            sems[sname] = outer.enter_context(nc.semaphore(sname))
        SEMMAP = {("cv" if n == "cv_" else n): n for n in SEMNAMES}

        def dma(queue, sem, dst, src, waits=(), slow=False):
            return E(queue, lambda e, d=dst, s=src, sl=slow: e.dma_start(
                out=d, in_=s, allow_slow_non_contiguous=sl),
                waits=waits, incs=[(sem, 16)])[sem]

        def dmat(queue, sem, dst, src, waits=()):
            return E(queue, lambda e, d=dst, s=src: e.dma_start_transpose(d, s),
                     waits=waits, incs=[(sem, 16)])[sem]

        def tap(name, src, waits, queue="sync"):
            if name in tap_out:
                d = dma(queue, "q_tap", tap_out[name][:, :], src, waits=waits)
                for eng in ("vector", "scalar", "tensor"):
                    E(eng, None, waits=[("q_tap", d)])

        def ap1(t, n, m):
            return bass.AP(t.tensor, 0, [[1, n], [n, m]])

        dma("sync", "q_const", cc[:, :], consts[:, :])
        dma("sync", "q_const", ones_bf[:, :], onesbf[:, :])
        dma("sync", "q_const", sel_bf[:, :], selbf[:, :])
        dma("sync", "q_const", ones_f[:, :], onesf[:, :])
        dma("sync", "q_const", ones_fr[:, :], onesfr[:, :])
        dma("sync", "q_const", idn[:, :], ident[:, :])
        dma("sync", "q_const", binp[:, :], ap1(b_in, 128, 16), slow=True)
        dma("sync", "q_const", bxpu[:, :], ap1(bxp_u, 128, KD), slow=True)
        dma("sync", "q_const", bxpds[:, :], bass.AP(bxp_ds.tensor, 0, [[1, 16], [16, 1]]), slow=True)
        dma("sync", "q_const", bxpb[:, :], bass.AP(bxp_b.tensor, 0, [[1, 16], [16, 1]]), slow=True)
        dma("sync", "q_const", dtb_t[:, :], ap1(dtb, 128, KD), slow=True)
        dma("sync", "q_const", cwt[:, :], bass.AP(convw.tensor, 0, [[4, 128], [512, KD], [1, 4]]), slow=True)
        dma("sync", "q_const", cbt[:, :], ap1(convb, 128, KD), slow=True)
        dma("sync", "q_const", dvt[:, :], ap1(dvec, 128, KD), slow=True)
        ld0 = dma("sync", "q_const", wstg[0][0:1, 0:DM], bass.AP(bout.tensor, 0, [[DM, 1], [1, DM]]))
        tap("cwt", cwt[:, :], [("q_const", S.now("q_const"))])
        tap("cbt", cbt[:, :], [("q_const", S.now("q_const"))])

        w_brow = E("scalar", lambda e: e.copy(boutrow[:, :], wstg[0][0:1, 0:DM]),
                   waits=[("q_const", ld0)], incs=[("cs", 1)])["cs"]
        tb = E("tensor", lambda e: e.matmul(p_mm[0][:, :], ones_bf[:, :],
                                            boutrow[0:1, 0:TC], start=True, stop=True),
               waits=[("cs", w_brow)], incs=[("ct", 1)])["ct"]
        tb2 = E("tensor", lambda e: e.matmul(p_mm[1][:, :], ones_bf[:, :],
                                             boutrow[0:1, TC:DM], start=True, stop=True),
                incs=[("ct", 1)])["ct"]
        E("scalar", lambda e: e.copy(boutb[:, 0:TC], p_mm[0][:, :]),
          waits=[("ct", tb)], incs=[("cs", 1)])
        evb = E("scalar", lambda e: e.copy(boutb[:, TC:DM], p_mm[1][:, :]),
                waits=[("ct", tb2)], incs=[("cs", 1)])["cs"]

        mix = ExitStack()

        def sbm(name, shape, dtype=F32):
            return mix.enter_context(nc.sbuf_tensor(name, shape, dtype))

        win_b = [sbm(f"win_b{k}", [128, DI], BF16) for k in range(KM)]
        wxp_b = [sbm(f"wxp_b{k}", [128, 2112], BF16) for k in range(KD)]
        wout_b = [sbm(f"wout_b{k}", [128, DM], BF16) for k in range(KD)]
        dtw_b = sbm("dtw_b", [16, DH], BF16)
        xtm = [sbm("xtm0", [128, DM], BF16)]
        xfm = [sbm(f"xfm{k}", [128, TC], BF16) for k in range(KM)]
        xc = [sbm(f"xc{m}", [128, TC + 3], BF16) for m in range(KD)]
        cv_t = sbm("cv_t", [128, TC], BF16)
        zs = [sbm(f"zs{m}", [128, TC], BF16) for m in range(KD)]
        ut = [sbm(f"ut{m}", [128, TC], BF16) for m in range(KD)]
        dl = [sbm("dl", [128, TC])]
        dlb = [sbm("dlb", [128, TC], BF16)]
        udt = dlb[0]
        dbfr = sbm("dbfr", [48, TC])
        dsb_bf = sbm("dsb_bf", [16, TC], BF16)
        bbf = sbm("bbf", [16, TC], BF16)
        bbc = [sbm(f"bbc{s}", [128, TC], BF16) for s in range(NS)]
        relu_t = sbm("relu_t", [128, TC])
        abs_t = sbm("abs_t", [128, TC])
        exp_t = sbm("exp_t", [128, TC])
        ln_t = abs_t
        vbf = sbm("vbf", [128, TC], BF16)
        a_rot = [sbm(f"a_rot{i}", [128, TC]) for i in range(2)]
        b_rot = [sbm("b_rot", [128, TC])]
        h_all = sbm("h_all", [128, NS * TC], BF16)
        f1 = sbm("f1", [128, NS * TC // 2], BF16)
        yb = [sbm(f"yb{m}", [128, TC], BF16) for m in range(KD)]
        f2 = h_all[:, 0:NS * TC // 4]
        f3 = f1[:, 0:NS * TC // 8]
        hsum = f1[:, NS * TC // 8: NS * TC // 8 + TC]
        xcs = xfm

        def load_weights(dst_tiles, src, rows, cols, startw):
            evs = [startw]
            pieces = [(o, min(1056, cols - o)) for o in range(0, cols, 1056)]
            i = 0
            for k in range(rows // 128):
                for (o, wdt) in pieces:
                    w = [("cs", evs[-2])] if len(evs) >= 2 else [("cs", evs[-1])]
                    d = dma("sync", f"q_wl{i % 2}", wstg[i % 2][:, 0:wdt],
                            src[k * 128:(k + 1) * 128, o:o + wdt], waits=w)
                    evs.append(E("scalar", lambda e, k=k, o=o, wdt=wdt, i=i: e.copy(
                        dst_tiles[k][:, o:o + wdt], wstg[i % 2][:, 0:wdt]),
                        waits=[(f"q_wl{i % 2}", d)], incs=[("cs", 1)])["cs"])
                    i += 1
            return evs[-1]

        lw = load_weights(win_b, win, DM, DI, evb)
        lw = load_weights(wxp_b, wxp, DH, 2112, lw)
        lw = load_weights(wout_b, wout, DH, DM, lw)
        d_dtw = dma("sync", "q_wl0", wstg[0][0:16, 0:DH], dtw[:, :], waits=[("cs", lw)])
        WREADY = E("scalar", lambda e: e.copy(dtw_b[:, :], wstg[0][0:16, 0:DH]),
                   waits=[("q_wl0", d_dtw)], incs=[("cs", 1)])["cs"]

        last_store = {}
        rs2_ready = {}
        scan_prev = {}
        vv_guard = {}
        udt_guard = [None]
        prev_ag = [None]
        prev_sil = {}
        prev_dl = None
        slot_last = [None, None, None, None]

        for c in range(NCH):
            base = c * TC
            tr_ev = []
            for j in range(4):
                w = [("cs", WREADY)] if c == 0 and j == 0 else []
                if j >= 1:
                    w.append(("cs", tr_ev[-1]))
                if c > 0 and j == 0:
                    w.append(("cv", S.now("cv")))
                xl = dma("gpsimd", "q_x", xtm[0][:, :],
                         xfull[base + j * 128: base + (j + 1) * 128, :], waits=w)
                for f in range(KM):
                    wt = [("q_x", xl)]
                    if tr_ev:
                        wt.append(("cs", tr_ev[-1]))
                    tt = E("tensor", lambda e, f=f: e.transpose(
                        p_tr[:, 0:128], xtm[0][:, f * 128:(f + 1) * 128], idn[:, :]),
                        waits=wt, incs=[("ct", 1)])["ct"]
                    tr_ev.append(E("scalar", lambda e, j=j, f=f: e.copy(
                        xfm[f][:, j * 128:(j + 1) * 128], p_tr[:, 0:128]),
                        waits=[("ct", tt)], incs=[("cs", 1)])["cs"])
            xfm_done = tr_ev[-1]

            halo = []
            for m in range(KD):
                if c == 0:
                    halo.append(E("vector", lambda e, m=m: e.memset(xc[m][:, 0:3], 0.0),
                                  incs=[("cv", 1)])["cv"])
                else:
                    halo.append(E("vector", lambda e, m=m: e.tensor_copy(
                        xc[m][:, 0:3], xc[m][:, TC:TC + 3]),
                        waits=[("cs", prev_sil[m])], incs=[("cv", 1)])["cv"])

            ev_mm = {}
            for m in range(16):
                bank = m % 2
                for k in range(KM):
                    w = []
                    if k == 0:
                        w = [("cs", xfm_done)]
                        if m >= 2:
                            w.append(("cs", ev_mm[m - 2]))
                    E("tensor", lambda e, k=k, m=m, bank=bank: e.matmul(
                        p_mm[bank][:, :], win_b[k][:, m * 128:(m + 1) * 128],
                        xfm[k][:, :], start=(k == 0), stop=(k == KM - 1)),
                      waits=w, incs=([("ct", 1)] if k == KM - 1 else ()))
                mm_t = S.now("ct")
                if m < KD:
                    ev_mm[m] = E("scalar", lambda e, m=m, bank=bank: e.activation(
                        xc[m][:, 3:TC + 3], p_mm[bank][:, :], AF.Identity,
                        bias=binp[:, m:m + 1]),
                        waits=[("ct", mm_t), ("cv", halo[m])], incs=[("cs", 1)])["cs"]
                else:
                    ev_mm[m] = E("scalar", lambda e, m=m, bank=bank: e.activation(
                        zs[m - KD][:, :], p_mm[bank][:, :], AF.Silu,
                        bias=binp[:, m:m + 1]),
                        waits=[("ct", mm_t)], incs=[("cs", 1)])["cs"]

            if c == 0:
                tap("xc0", xc[0][:, 3:TC + 3], [("cs", ev_mm[0])], queue="gpsimd")
                tap("z0", zs[0][:, :], [("cs", ev_mm[KD])], queue="gpsimd")
                tap("xfm0", xfm[0][:, :], [("cs", xfm_done)], queue="gpsimd")
            sil = {}
            for m in range(KD):
                wv1 = [("cs", ev_mm[15])]
                if m > 0:
                    wv1.append(("cs", sil[m - 1]))
                v1 = E("vector", lambda e, m=m: e.tensor_scalar(
                    cv_t[:, :], xc[m][:, 0:TC], cwt[:, 4 * m:4 * m + 1], None, OP.mult),
                    waits=wv1, incs=[("cv", 1)])["cv"]
                for kk in range(1, 4):
                    v1 = E("vector", lambda e, m=m, kk=kk: e.scalar_tensor_tensor(
                        cv_t[:, :], xc[m][:, kk:TC + kk], cwt[:, 4 * m + kk:4 * m + kk + 1],
                        cv_t[:, :], OP.mult, OP.add), incs=[("cv", 1)])["cv"]
                if c == 0 and m == 0:
                    tap("cv0", cv_t[:, :], [("cv", v1)], queue="gpsimd")
                    tap("xch0", xc[0][:, :], [("cv", v1)], queue="gpsimd")
                sil[m] = E("scalar", lambda e, m=m: e.activation(
                    xcs[m][:, :], cv_t[:, :], AF.Silu, bias=cbt[:, m:m + 1]),
                    waits=[("cv", v1)], incs=[("cs", 1)])["cs"]
            prev_sil = sil

            if c == 0:
                tap("xcs0", xcs[0][:, :], [("cs", sil[0])], queue="gpsimd")
            mcols = [(0, 32)] + [(32 + i * 128, 128) for i in range(8)] + \
                    [(1056, 32)] + [(1088 + i * 128, 128) for i in range(8)]
            stg_dma = []
            stg_ev = []
            for mi, (c0, width) in enumerate(mcols):
                bank = mi % 2
                for k in range(KD):
                    w = []
                    if k == 0:
                        w = [("cs", sil[KD - 1])]
                        if mi >= 2:
                            w.append(("cs", stg_ev[mi - 2]))
                    E("tensor", lambda e, k=k, c0=c0, width=width, bank=bank: e.matmul(
                        p_mm[bank][0:width, :], wxp_b[k][:, c0:c0 + width],
                        xcs[k][:, :], start=(k == 0), stop=(k == KD - 1)),
                      waits=w, incs=([("ct", 1)] if k == KD - 1 else ()))
                mm_t = S.now("ct")
                sg = mi % 4
                wv = [("ct", mm_t)]
                if slot_last[sg] is not None:
                    wv.append((f"q_s{sg}", slot_last[sg]))
                ev = E("scalar", lambda e, width=width, sg=sg, bank=bank: e.copy(
                    wstg[sg % 2][0:width, (sg // 2) * TC:(sg // 2 + 1) * TC],
                    p_mm[bank][0:width, :]), waits=wv, incs=[("cs", 1)])["cs"]
                stg_ev.append(ev)
                slot_last[sg] = dma("sync", f"q_s{sg}", cc1_in[c][c0:c0 + width, :],
                                    wstg[sg % 2][0:width, (sg // 2) * TC:(sg // 2 + 1) * TC],
                                    waits=[("cs", ev)])
                stg_dma.append(slot_last[sg])

            rs1 = E("gpsimd", lambda e, c=c: e.collective_compute(
                "ReduceScatter", OP.add, replica_groups=GROUPS,
                ins=[cc1_in[c][:, :]], outs=[cc1_out[c][:, :]]),
                waits=[(f"q_s{i}", S.now(f"q_s{i}")) for i in range(4)],
                incs=[("cc1", 1)])["cc1"]
            d_dsb = dma("gpsimd", "q_u", dbfr[0:16, :], cc1_out[c][0:16, :],
                        waits=[("cc1", rs1)])
            d_bfr = dma("gpsimd", "q_u", dbfr[32:48, :], cc1_out[c][16:32, :])
            d_u = []
            for i in range(KD):
                w = [("cv", S.now("cv"))] if (c > 0 and i == 0) else []
                d_u.append(dma("gpsimd", "q_u", ut[i][:, :],
                               cc1_out[c][32 + i * 128:32 + (i + 1) * 128, :], waits=w))
            u_all = S.now("q_u")
            bias_u = []
            for i in range(KD):
                bias_u.append(E("vector", lambda e, i=i: e.tensor_scalar(
                    ut[i][:, :], ut[i][:, :], bxpu[:, i:i + 1], None, OP.add),
                    waits=[("q_u", u_all)], incs=[("cv", 1)])["cv"])
            bias_db = E("vector", lambda e: e.tensor_scalar(
                dbfr[0:16, :], dbfr[0:16, :], bxpds[:, 0:1], None, OP.add),
                waits=[("q_u", u_all)], incs=[("cv", 1)])["cv"]
            bias_b = E("vector", lambda e: e.tensor_scalar(
                dbfr[32:48, :], dbfr[32:48, :], bxpb[:, 0:1], None, OP.add),
                waits=[("q_u", u_all)], incs=[("cv", 1)])["cv"]
            c_dsbf = E("scalar", lambda e: e.copy(dsb_bf[:, :], dbfr[0:16, :]),
                       waits=[("cv", bias_db)], incs=[("cs", 1)])["cs"]
            c_bbf = E("scalar", lambda e: e.copy(bbf[:, :], dbfr[32:48, :]),
                      waits=[("cv", bias_b)], incs=[("cs", 1)])["cs"]

            if c == 0:
                tap("ds0", dbfr[0:16, :], [("cv", bias_db)])
                tap("b0", dbfr[32:48, :], [("cv", bias_b)])
                tap("u0", ut[0][:, :], [("cv", bias_u[0])], queue="gpsimd")
            bb_ev = {}
            for s in range(NS):
                bank = s % 2
                w = [("cs", c_bbf)]
                if s >= 2:
                    w.append(("cs", bb_ev[s - 2]))
                if c > 0 and s < 2:
                    w.append(("cv", scan_prev.get("bbc", 0)))
                tt = E("tensor", lambda e, s=s, bank=bank: e.matmul(
                    p_bc[bank][:, :], sel_bf[:, s * 128:(s + 1) * 128], bbf[:, :],
                    start=True, stop=True), waits=w, incs=[("ct", 1)])["ct"]
                bb_ev[s] = E("scalar", lambda e, s=s, bank=bank: e.copy(
                    bbc[s][:, :], p_bc[bank][:, :]),
                    waits=[("ct", tt)], incs=[("cs", 1)])["cs"]

            y_done = {}
            dl_ev = {}
            for k in range(KD):
                bank = k % 2
                w = [("cs", c_dsbf)]
                if k >= 2:
                    w.append(("cv", dl_ev[k - 2]))
                tt = E("tensor", lambda e, k=k, bank=bank: e.matmul(
                    p_mm[bank][:, :], dtw_b[:, k * 128:(k + 1) * 128], dsb_bf[:, :],
                    start=True, stop=True), waits=w, incs=[("ct", 1)])["ct"]
                wr1 = [("ct", tt)]
                if k > 0:
                    wr1.append(("cv", dl_ev[k - 1]))
                elif prev_dl is not None:
                    wr1.append(("cv", prev_dl))
                r1 = E("scalar", lambda e, bank=bank, k=k: e.activation(
                    relu_t[:, :], p_mm[bank][:, :], AF.Relu, bias=dtb_t[:, k:k + 1]),
                    waits=wr1, incs=[("cs", 1)])["cs"]
                E("scalar", lambda e, bank=bank, k=k: e.activation(
                    abs_t[:, :], p_mm[bank][:, :], AF.Abs, bias=dtb_t[:, k:k + 1]),
                  incs=[("cs", 1)])
                E("scalar", lambda e: e.activation(
                    exp_t[:, :], abs_t[:, :], AF.Exp, scale=-1.0), incs=[("cs", 1)])
                r4 = E("scalar", lambda e: e.activation(
                    ln_t[:, :], exp_t[:, :], AF.Ln, bias=cc[:, C_ONE:C_ONE + 1]),
                    incs=[("cs", 1)])["cs"]
                dl_ev[k] = E("vector", lambda e, k=k: e.tensor_tensor(
                    dl[0][:, :], relu_t[:, :], ln_t[:, :], OP.add),
                    waits=[("cs", r4)] + ([("cs", prev_ag[0])] if prev_ag[0] is not None else []), incs=[("cv", 1)])["cv"]
                prev_dl = dl_ev[k]
                if c == 0 and k == 0:
                    tap("dl0", dl[0][:, :], [("cv", dl_ev[0])], queue="gpsimd")
                wdlb = [("cv", dl_ev[k])]
                if vv_guard.get(0) is not None:
                    wdlb.append(("cv", vv_guard[0]))
                if udt_guard[0] is not None:
                    wdlb.append(("cv", udt_guard[0]))
                dlb_ev = E("scalar", lambda e, k=k: e.copy(
                    dlb[0][:, :], dl[0][:, :]),
                    waits=wdlb, incs=[("cs", 1)])["cs"]
                vv = E("vector", lambda e, k=k: e.tensor_tensor(
                    vbf[:, :], dlb[0][:, :], ut[k][:, :], OP.mult),
                    waits=[("cv", bias_u[k]), ("cs", dlb_ev),
                           ("cs", bb_ev[NS - 1])], incs=[("cv", 1)])["cv"]
                vv_guard[0] = vv
                ag = []
                sc_last = None
                for s in range(NS):
                    w = []
                    if s == 0:
                        w.append(("cv", dl_ev[k]))
                    if (s % 2) in scan_prev:
                        w.append(("cv", scan_prev[s % 2]))
                    ag.append(E("scalar", lambda e, k=k, s=s: e.activation(
                        a_rot[s % 2][:, :], dl[0][:, :], AF.Exp,
                        scale=cc[:, C_NEGS + s:C_NEGS + s + 1]),
                        waits=w, incs=[("cs", 1)])["cs"])
                    bE = E("vector", lambda e, s=s: e.tensor_tensor(
                        b_rot[0][:, :], vbf[:, :], bbc[s][:, :], OP.mult),
                        waits=[("cv", vv)] if s == 0 else [], incs=[("cv", 1)])["cv"]
                    if c == 0:
                        init = 0.0
                    else:
                        init = carry[:, k * NS + s:k * NS + s + 1]
                    sc = E("vector", lambda e, s=s, init=init: e.tensor_tensor_scan(
                        h_all[:, s * TC:(s + 1) * TC], a_rot[s % 2][:, :],
                        b_rot[0][:, :], init, OP.mult, OP.add),
                        waits=[("cs", ag[s])], incs=[("cv", 1)])["cv"]
                    scan_prev[s % 2] = sc
                    sc_last = sc
                prev_ag[0] = ag[NS - 1]
                scan_prev["bbc"] = sc_last
                if c == 0 and k == 0:
                    tap("h0", h_all[:, 0:TC], [("cv", sc_last)], queue="gpsimd")
                    tap("h15", h_all[:, 15 * TC:16 * TC], [("cv", sc_last)], queue="gpsimd")
                    tap("a0", a_rot[0][:, :], [("cs", ag[0])], queue="gpsimd")
                    tap("b_0", b_rot[0][:, :], [("cv", sc_last)], queue="gpsimd")
                hre = h_all[:, :].rearrange("p (s t) -> p s t", s=NS)
                cre = carry[:, k * NS:(k + 1) * NS].rearrange("p (s o) -> p s o", o=1)
                cs_ev = E("scalar", lambda e, hre=hre, cre=cre: e.copy(
                    cre, hre[:, :, TC - 1:TC]),
                    waits=[("cv", sc_last)], incs=[("cs", 1)])["cs"]
                E("vector", lambda e: e.tensor_tensor(
                    f1[:, :], h_all[:, 0:NS * TC // 2], h_all[:, NS * TC // 2:NS * TC],
                    OP.add), waits=[("cs", cs_ev)], incs=[("cv", 1)])
                E("vector", lambda e: e.tensor_tensor(
                    f2, f1[:, 0:NS * TC // 4], f1[:, NS * TC // 4:NS * TC // 2],
                    OP.add), incs=[("cv", 1)])
                E("vector", lambda e: e.tensor_tensor(
                    f3, f2[:, 0:NS * TC // 8], f2[:, NS * TC // 8:NS * TC // 4],
                    OP.add), incs=[("cv", 1)])
                E("vector", lambda e: e.tensor_tensor(
                    hsum, f3[:, 0:TC], f3[:, TC:2 * TC], OP.add), incs=[("cv", 1)])
                wud = [("cv", bias_u[k]), ("cv", vv_guard[0])]
                if udt_guard[0] is not None:
                    wud.append(("cv", udt_guard[0]))
                ud_ev = E("scalar", lambda e, k=k: e.activation(
                    udt[:, :], ut[k][:, :], AF.Identity, scale=dvt[:, k:k + 1]),
                    waits=wud, incs=[("cs", 1)])["cs"]
                udt_guard[0] = E("vector", lambda e: e.tensor_tensor(
                    hsum, hsum, udt[:, :], OP.add),
                    waits=[("cs", ud_ev)], incs=[("cv", 1)])["cv"]
                y_done[k] = E("vector", lambda e, k=k: e.tensor_tensor(
                    yb[k][:, :], hsum, zs[k][:, :], OP.mult), incs=[("cv", 1)])["cv"]

            if c == 0:
                tap("y0", yb[0][:, :], [("cv", y_done[0])], queue="gpsimd")
                tap("y7", yb[7][:, :], [("cv", y_done[7])], queue="gpsimd")
            op_ev = []
            for j in range(4):
                for n in range(2):
                    for k in range(KD):
                        w = [("cv", y_done[k])]
                        if k == 0 and len(op_ev) >= 2:
                            w.append(("cs", op_ev[-2]))
                        E("tensor", lambda e, j=j, n=n, k=k: e.matmul(
                            p_op[n][:, :], yb[k][:, j * 128:(j + 1) * 128],
                            wout_b[k][:, n * TC:(n + 1) * TC],
                            start=(k == 0), stop=(k == KD - 1)),
                          waits=w, incs=([("ct", 1)] if k == KD - 1 else ()))
                    mm_t = S.now("ct")
                    sg = (2 * j + n) % 4
                    wv = [("ct", mm_t), (f"q_s{sg}", slot_last[sg])]
                    ev = E("scalar", lambda e, n=n, sg=sg: e.copy(
                        wstg[sg % 2][:, (sg // 2) * TC:(sg // 2 + 1) * TC],
                        p_op[n][:, :]), waits=wv, incs=[("cs", 1)])["cs"]
                    op_ev.append(ev)
                    if c == 0 and j == 0 and n == 0:
                        tap("op00", wstg[0][:, 0:TC], [("cs", ev)])
                    half = 0 if c < 4 else 1
                    cp = c % 4
                    slot_last[sg] = dma("sync", f"q_s{sg}",
                                        cc2_in[cp][half * TC + j * 128: half * TC + (j + 1) * 128,
                                                   n * TC:(n + 1) * TC],
                                        wstg[sg % 2][:, (sg // 2) * TC:(sg // 2 + 1) * TC],
                                        waits=[("cs", ev)])
            if c == 0:
                tap("pin", cc2_in[0][0:128, :],
                    [(f"q_s{i}", S.now(f"q_s{i}")) for i in range(4)])
            if c == 4:
                tap("pin4", cc2_in[0][TC:TC + 128, :],
                    [(f"q_s{i}", S.now(f"q_s{i}")) for i in range(4)])
            if c >= 4:
                rs2_ready[c - 4] = E("gpsimd", lambda e, i=c - 4: e.collective_compute(
                    "ReduceScatter", OP.add, replica_groups=GROUPS,
                    ins=[cc2_in[i][:, :]], outs=[cc2_out[i][:, :]]),
                    waits=[(f"q_s{i}", S.now(f"q_s{i}")) for i in range(4)],
                    incs=[("cc2", 1)])["cc2"]
                if c == 4:
                    tap("mixraw", cc2_out[0][0:128, :], [("cc2", rs2_ready[0])])

        MIXDONE = S.now("cc2")
        mix.close()

        ffn = ExitStack()

        def sbf(name, shape, dtype=F32):
            return ffn.enter_context(nc.sbuf_tensor(name, shape, dtype))

        qscope = ExitStack()
        wq_t = qscope.enter_context(nc.sbuf_tensor("wq_t", [128, INNER], F32))
        wq_t2 = qscope.enter_context(nc.sbuf_tensor("wq_t2", [128, INNER], F32))
        wq_i = qscope.enter_context(nc.sbuf_tensor("wq_i", [128, INNER], I32))
        wq_bf = qscope.enter_context(nc.sbuf_tensor("wq_bf", [128, INNER], BF16))
        trp = [qscope.enter_context(nc.sbuf_tensor(f"trp{i}", [128, 128], BF16)) for i in range(2)]
        wsum = qscope.enter_context(nc.sbuf_tensor("wsum", [128, 32], F32))
        wcol = qscope.enter_context(nc.sbuf_tensor("wcol", [128, 4], F32))

        def quant_weight(src, rows, cols, dstT, mclip_idx, startw):
            nrt = rows // 128
            evs = []
            for r in range(nrt):
                w = [("cv", evs[-1])] if evs else list(startw)
                d = dma("sync", "q_wq", wq_t[:, 0:cols], src[r * 128:(r + 1) * 128, :],
                        waits=w)
                evs.append(E("vector", lambda e, r=r, cols=cols: e.tensor_reduce(
                    wsum[:, r:r + 1], wq_t[:, 0:cols], AX.X, OP.add,
                    apply_absolute_value=True),
                    waits=[("q_wq", d)], incs=[("cv", 1)])["cv"])
            sb = E("scalar", lambda e, nrt=nrt: e.copy(
                wq_t2[:, 0:nrt], wsum[:, 0:nrt]),
                waits=[("cv", evs[-1])], incs=[("cs", 1)])["cs"]
            rsum = E("vector", lambda e, nrt=nrt: e.tensor_reduce(
                wcol[:, 0:1], wq_t2[:, 0:nrt], AX.X, OP.add),
                waits=[("cs", sb)], incs=[("cv", 1)])["cv"]
            if mclip_idx == 2:
                tap("wdsum", wsum[:, 0:8], [("cv", rsum)])
                tap("wdcol", wcol[:, 0:4], [("cv", rsum)])
            d1 = dma("sync", "q_wq", scr_w[:, :], wcol[:, 0:1], waits=[("cv", rsum)])
            d2 = dma("sync", "q_wq", wq_t2[0:1, 0:128],
                     bass.AP(scr_w.tensor, 0, [[128, 1], [1, 128]]),
                     waits=[("q_wq", d1)])
            m0 = E("vector", lambda e: e.tensor_reduce(
                wcol[0:1, 0:1], wq_t2[0:1, 0:128], AX.X, OP.add),
                waits=[("q_wq", d2)], incs=[("cv", 1)])["cv"]
            if mclip_idx == 2:
                tap("wdrow", wq_t2[0:128, 0:128], [("cv", m0)])
            d3 = dma("sync", "q_wq", scr_m[:, 0:1], wcol[0:1, 0:1], waits=[("cv", m0)])
            d4 = dma("sync", "q_wq", wcol[:, 0:1],
                     bass.AP(scr_m.tensor, 0, [[0, 128], [1, 1]]),
                     waits=[("q_wq", d3)])
            mm1 = E("vector", lambda e, rows=rows, cols=cols: e.tensor_scalar(
                mclipc[:, mclip_idx:mclip_idx + 1], wcol[:, 0:1],
                1.0 / (rows * cols), 1e-5, OP.mult, OP.max),
                waits=[("q_wq", d4)], incs=[("cv", 1)])["cv"]
            mm1b = E("scalar", lambda e: e.copy(
                wcol[:, 1:2], mclipc[:, mclip_idx:mclip_idx + 1]),
                waits=[("cv", mm1)], incs=[("cs", 1)])["cs"]
            eb2v = E("vector", lambda e: e.reciprocal(wcol[:, 3:4], wcol[:, 1:2]),
                     waits=[("cs", mm1b)], incs=[("cv", 1)])["cv"]
            eb2 = E("scalar", lambda e: e.copy(wcol[:, 2:3], wcol[:, 3:4]),
                    waits=[("cv", eb2v)], incs=[("cs", 1)])["cs"]
            prev_q2 = None
            tslot = [None, None]
            for r in range(nrt):
                w = [("cs", eb2)] if r == 0 else [("cv", prev_q2)]
                d = dma("sync", "q_wq", wq_t[:, 0:cols], src[r * 128:(r + 1) * 128, :],
                        waits=w)
                E("vector", lambda e, cols=cols: e.tensor_scalar(
                    wq_t2[:, 0:cols], wq_t[:, 0:cols], wcol[:, 2:3], None,
                    OP.mult), waits=[("q_wq", d)], incs=[("cv", 1)])
                q2 = E("vector", lambda e, cols=cols: e.tensor_scalar(
                    wq_t2[:, 0:cols], wq_t2[:, 0:cols], -1.0, 1.0,
                    OP.max, OP.min), incs=[("cv", 1)])["cv"]
                prev_q2 = q2
                wq3 = [("cv", q2)]
                for ts_ in (0, 1):
                    if tslot[ts_] is not None:
                        wq3.append((f"q_t{ts_}", tslot[ts_][0]))
                q3 = E("scalar", lambda e, cols=cols: e.activation(
                    wq_i[:, 0:cols], wq_t2[:, 0:cols], AF.Identity),
                    waits=wq3, incs=[("cs", 1)])["cs"]
                q4 = E("scalar", lambda e, cols=cols: e.copy(wq_bf[:, 0:cols],
                                                             wq_i[:, 0:cols]),
                       incs=[("cs", 1)])["cs"]
                for kb in range(cols // 128):
                    sl = kb % 2
                    wt = [("cs", q4)]
                    if tslot[sl] is not None:
                        wt.append((f"q_t{sl}", tslot[sl][0]))
                    tr = dmat("scalar", f"q_t{sl}", trp[sl][:, :],
                              wq_bf[:, kb * 128:(kb + 1) * 128], waits=wt)
                    out_d = dma("sync", f"q_t{sl}",
                                dstT[(kb * nrt + r) * 128:(kb * nrt + r + 1) * 128, :],
                                trp[sl][:, :], waits=[(f"q_t{sl}", tr)])
                    tslot[sl] = (out_d, tr)
            return (S.now("q_t0"), S.now("q_t1"))

        qs = quant_weight(wg, INNER, DM, wgT_d, 0, [("cc2", MIXDONE)])
        qs = quant_weight(wu, INNER, DM, wuT_d, 1,
                          [("q_t0", qs[0]), ("q_t1", qs[1])])
        qs = quant_weight(wd, DM, INNER, wdT_d, 2,
                          [("q_t0", qs[0]), ("q_t1", qs[1])])
        WQDONE = [("q_t0", qs[0]), ("q_t1", qs[1])]
        qscope.close()
        xres = [sbf("xres0", [128, DM])]
        xln = [sbf(f"xln{j}", [128, DM]) for j in range(4)]
        outt = [sbf(f"outt{j}", [128, DM]) for j in range(4)]
        mixt = outt
        hf = [sbf(f"hf{j}", [128, INNER], BF16) for j in range(4)]
        dump = sbf("dump", [128, DM])
        qint = sbf("qint", [128, TC], I32)
        xq_b = sbf("xq_b", [128, DM], BF16)
        xqT = [sbf(f"xqT{k}", [128, TC], BF16) for k in range(KM)]
        hq_b = [sbf(f"hq_b{i}", [128, TC], BF16) for i in range(2)]
        hqT = [sbf(f"hqT{k}", [128, TC], BF16) for k in range(KI)]
        wgl = [sbf(f"wgl{k}", [128, TC], BF16) for k in range(KM)]
        wul = [sbf(f"wul{k}", [128, TC], BF16) for k in range(KM)]
        wdl = [sbf(f"wdl{k}", [128, TC], BF16) for k in range(16)]
        dqg = sbf("dqg", [128, 4])
        dqu = sbf("dqu", [128, 4])
        dqd = sbf("dqd", [128, 4])
        relu2 = sbf("relu2", [128, TC])
        abs2 = sbf("abs2", [128, TC])


        lntapped = []

        def layernorm(eng_src_tile, j, wait_cv):
            s2 = E("scalar", lambda e: e.activation(
                dump[:, :], eng_src_tile[:, :], AF.Square),
                waits=[("cv", wait_cv)], incs=[("cs", 1)])["cs"]
            E("vector", lambda e: e.tensor_reduce(
                col[:, 0:1], eng_src_tile[:, :], AX.X, OP.add),
              waits=[("cv", wait_cv)], incs=[("cv", 1)])
            E("vector", lambda e: e.tensor_reduce(
                col[:, 1:2], dump[:, :], AX.X, OP.add),
              waits=[("cs", s2)], incs=[("cv", 1)])
            E("vector", lambda e: e.tensor_scalar(
                col[:, 2:3], col[:, 0:1], 1.0 / DM, None, OP.mult),
              incs=[("cv", 1)])
            E("vector", lambda e: e.tensor_scalar(
                col[:, 3:4], col[:, 1:2], 1.0 / DM, None, OP.mult), incs=[("cv", 1)])
            v4t = S.now("cv")
            E("scalar", lambda e: e.activation(
                col[:, 4:5], col[:, 2:3], AF.Square),
              waits=[("cv", v4t)], incs=[("cs", 1)])
            E("scalar", lambda e: e.activation(
                col[:, 5:6], col[:, 4:5], AF.Identity, scale=-1.0,
                bias=col[:, 3:4]), incs=[("cs", 1)])
            s3 = E("scalar", lambda e: e.activation(
                col[:, 6:7], col[:, 5:6], AF.Sqrt, bias=cc[:, C_EPS:C_EPS + 1]),
                incs=[("cs", 1)])["cs"]
            if not lntapped and "lncol" in tap_out:
                lntapped.append(1)
                tap("lncol", col[:, :], [("cs", s3)])
            E("vector", lambda e: e.reciprocal(col[:, 7:8], col[:, 6:7]),
              waits=[("cs", s3)], incs=[("cv", 1)])
            return E("vector", lambda e: e.tensor_scalar(
                eng_src_tile[:, :], eng_src_tile[:, :], col[:, 2:3], col[:, 7:8],
                OP.subtract, OP.mult), incs=[("cv", 1)])["cv"]

        final_dma = None
        for i in range(NFC):
            if i == 0:
                tap("mclip", mclipc[:, :], list(WQDONE))
            lm = []
            for j in range(4):
                w = []
                if j == 0:
                    w = [("cc2", MIXDONE)] + list(WQDONE)
                    if i > 0:
                        w.append(("cv", S.now("cv")))
                        w.append(("q_y", S.now("q_y")))
                lm.append(dma("sync", "q_f0", mixt[j][:, :],
                              cc2_out[i][j * 128:(j + 1) * 128, :], waits=w))
            ln_done = {}
            for j in range(4):
                wx = [("cv", S.now("cv"))] if (i > 0 or j > 0) else []
                xr = dma("sync", "q_f0", xres[0][:, :],
                         xmy[i * TC + j * 128: i * TC + (j + 1) * 128, :], waits=wx)
                v1 = E("vector", lambda e, j=j: e.tensor_tensor(
                    xln[j][:, :], mixt[j][:, :], xres[0][:, :], OP.add),
                    waits=[("q_f0", xr)], incs=[("cv", 1)])["cv"]
                v2 = E("vector", lambda e, j=j: e.tensor_tensor(
                    xln[j][:, :], xln[j][:, :], boutb[:, :], OP.add),
                    incs=[("cv", 1)])["cv"]
                if i == 0 and j == 0:
                    tap("mix0", mixt[0][:, :], [("q_f0", lm[0])])
                    tap("r0", xln[0][:, :], [("cv", v2)])
                ln_done[j] = layernorm(xln[j], j, v2)
                if i == 0 and j == 0:
                    tap("xln00", xln[0][:, :], [("cv", ln_done[0])])
                a1 = E("vector", lambda e, j=j: e.tensor_reduce(
                    col[:, 8:9], xln[j][:, :], AX.X, OP.max, apply_absolute_value=True),
                  waits=[("cs", S.now("cs"))], incs=[("cv", 1)])["cv"]
                a1b = E("scalar", lambda e: e.copy(col[:, 16:17], col[:, 8:9]),
                        waits=[("cv", a1)], incs=[("cs", 1)])["cs"]
                a3 = E("vector", lambda e: e.reciprocal(col[:, 9:10], col[:, 16:17]),
                       waits=[("cs", a1b)], incs=[("cv", 1)])["cv"]
                E("scalar", lambda e: e.activation(
                    col[:, 10:11], col[:, 9:10], AF.Identity, scale=127.0),
                  waits=[("cv", a3)], incs=[("cs", 1)])
                E("scalar", lambda e: e.activation(
                    col[:, 11:12], col[:, 16:17], AF.Identity, scale=1.0 / 127.0),
                  incs=[("cs", 1)])
                E("scalar", lambda e, j=j: e.activation(
                    dqg[:, j:j + 1], col[:, 11:12], AF.Identity,
                    scale=mclipc[:, 0:1]), incs=[("cs", 1)])
                a7s = E("scalar", lambda e, j=j: e.activation(
                    dqu[:, j:j + 1], col[:, 11:12], AF.Identity,
                    scale=mclipc[:, 1:2]), incs=[("cs", 1)])["cs"]
                a7 = a3
                if i == 0 and j == 0:
                    tap("qcol", col[:, :], [("cs", a7s)])
                for o in range(0, DM, TC):
                    E("scalar", lambda e, j=j, o=o: e.activation(
                        qint[:, :], xln[j][:, o:o + TC], AF.Identity,
                        scale=col[:, 10:11]),
                      waits=[("cv", a7)] + ([("q_xt", S.now("q_xt"))] if (j > 0 and o == 0) else []),
                      incs=[("cs", 1)])
                    E("scalar", lambda e, o=o: e.copy(xq_b[:, o:o + TC], qint[:, :]),
                      incs=[("cs", 1)])
                r2 = S.now("cs")
                for f in range(KM):
                    wt = [("cs", r2)]
                    if i > 0 and j == 0 and f == 0:
                        wt.append(("ct", S.now("ct")))
                    dmat("sync", "q_xt", xqT[f][:, j * 128:(j + 1) * 128],
                         xq_b[:, f * 128:(f + 1) * 128], waits=wt)
            xq_tr_done = S.now("q_xt")

            h_done = {}
            for n in range(KM):
                for k in range(KM):
                    w = [("ct", S.now("ct"))] if k == 0 else []
                    if i == 0 and n == 0 and k == 0:
                        w = w + list(WQDONE)
                    for q in range(4):
                        r_ = n * 4 + q
                        dma("gpsimd", "q_gw", wgl[k][:, q * 128:(q + 1) * 128],
                            wgT_d[(k * 32 + r_) * 128:(k * 32 + r_ + 1) * 128, :],
                            waits=(w if q == 0 else []))
                        dma("gpsimd", "q_gw", wul[k][:, q * 128:(q + 1) * 128],
                            wuT_d[(k * 32 + r_) * 128:(k * 32 + r_ + 1) * 128, :])
                wl_t = S.now("q_gw")
                if i == 0 and n == 0:
                    tap("wq0", wgl[0][:, :], [("q_gw", wl_t)], queue="gpsimd")
                    tap("xqT0", xqT[0][:, :], [("q_xt", xq_tr_done)], queue="gpsimd")
                for j in range(4):
                    for k in range(KM):
                        w = []
                        if k == 0:
                            w = [("q_gw", wl_t), ("q_xt", xq_tr_done),
                                 ("cv", S.now("cv"))]
                        E("tensor", lambda e, k=k, j=j: e.matmul(
                            p_op[0][:, :], xqT[k][:, j * 128:(j + 1) * 128],
                            wgl[k][:, :], start=(k == 0), stop=(k == KM - 1)),
                          waits=w, incs=([("ct", 1)] if k == KM - 1 else ()))
                    tg = S.now("ct")
                    for k in range(KM):
                        E("tensor", lambda e, k=k, j=j: e.matmul(
                            p_op[1][:, :], xqT[k][:, j * 128:(j + 1) * 128],
                            wul[k][:, :], start=(k == 0), stop=(k == KM - 1)),
                          incs=([("ct", 1)] if k == KM - 1 else ()))
                    tu = S.now("ct")
                    e1 = E("scalar", lambda e, j=j: e.activation(
                        relu2[:, :], p_op[0][:, :], AF.Silu, scale=dqg[:, j:j + 1]),
                        waits=[("ct", tg)], incs=[("cs", 1)])["cs"]
                    e2 = E("scalar", lambda e, j=j: e.activation(
                        abs2[:, :], p_op[1][:, :], AF.Identity, scale=dqu[:, j:j + 1]),
                        waits=[("ct", tu)], incs=[("cs", 1)])["cs"]
                    if i == 0 and n == 0 and j == 0:
                        tap("g00", relu2[:, :], [("cs", e1)])
                        tap("u00", abs2[:, :], [("cs", e2)])
                    h_done[(n, j)] = E("vector", lambda e, j=j, n=n: e.tensor_tensor(
                        hf[j][:, n * TC:(n + 1) * TC], relu2[:, :], abs2[:, :],
                        OP.mult), waits=[("cs", e2)], incs=[("cv", 1)])["cv"]

            if i == 0:
                tap("hf0", hf[0][:, 0:DM], [("cv", h_done[(1, 0)])], queue="gpsimd")
            hslot = [None, None]
            for j in range(4):
                w = [("cv", h_done[(KM - 1, j)]), ("cs", S.now("cs"))]
                b1 = E("vector", lambda e, j=j: e.tensor_reduce(
                    col[:, 12:13], hf[j][:, :], AX.X, OP.max, apply_absolute_value=True),
                  waits=w, incs=[("cv", 1)])["cv"]
                E("scalar", lambda e: e.activation(
                    col[:, 16:17], col[:, 12:13], AF.Relu,
                    bias=cc[:, C_NEPS:C_NEPS + 1]),
                  waits=[("cv", b1)], incs=[("cs", 1)])
                b2s = E("scalar", lambda e: e.activation(
                    col[:, 17:18], col[:, 16:17], AF.Identity,
                    bias=cc[:, C_EPS:C_EPS + 1]), incs=[("cs", 1)])["cs"]
                b3 = E("vector", lambda e: e.reciprocal(col[:, 13:14], col[:, 17:18]),
                       waits=[("cs", b2s)], incs=[("cv", 1)])["cv"]
                E("scalar", lambda e: e.activation(
                    col[:, 14:15], col[:, 13:14], AF.Identity, scale=127.0),
                  waits=[("cv", b3)], incs=[("cs", 1)])
                E("scalar", lambda e: e.activation(
                    col[:, 15:16], col[:, 17:18], AF.Identity, scale=1.0 / 127.0),
                  incs=[("cs", 1)])
                E("scalar", lambda e, j=j: e.activation(
                    dqd[:, j:j + 1], col[:, 15:16], AF.Identity,
                    scale=mclipc[:, 2:3]), incs=[("cs", 1)])
                b6 = b3
                for pi, o in enumerate(range(0, INNER, TC)):
                    buf = pi % 2
                    wq_ = [("cv", b6)]
                    if hslot[buf] is not None:
                        wq_.append((f"q_h{buf}", hslot[buf]))
                    E("scalar", lambda e, j=j, o=o: e.activation(
                        qint[:, :], hf[j][:, o:o + TC], AF.Identity,
                        scale=col[:, 14:15]), waits=wq_, incs=[("cs", 1)])
                    r2 = E("scalar", lambda e, buf=buf: e.copy(hq_b[buf][:, :],
                                                               qint[:, :]),
                           incs=[("cs", 1)])["cs"]
                    for fo in range(4):
                        f = o // 128 + fo
                        wt = [("cs", r2)]
                        if i > 0 and j == 0 and f == 0:
                            wt.append(("ct", S.now("ct")))
                        hslot[buf] = dmat("sync", f"q_h{buf}",
                                          hqT[f][:, j * 128:(j + 1) * 128],
                                          hq_b[buf][:, fo * 128:(fo + 1) * 128],
                                          waits=wt)
            hq_done = [("q_h0", S.now("q_h0")), ("q_h1", S.now("q_h1"))]

            p_dn = [p_op[0], p_op[1], p_mm[0], p_mm[1]]
            down_done = {}
            for n in range(2):
                for kh in range(2):
                    for kk in range(16):
                        w = [("ct", S.now("ct"))] if kk == 0 else []
                        if i == 0 and n == 0 and kh == 0 and kk == 0:
                            w = w + list(WQDONE)
                        for q in range(4):
                            r_ = n * 4 + q
                            kfull = kh * 16 + kk
                            dma("gpsimd", "q_gw", wdl[kk][:, q * 128:(q + 1) * 128],
                                wdT_d[(kfull * 8 + r_) * 128:(kfull * 8 + r_ + 1) * 128, :],
                                waits=(w if q == 0 else []))
                    wl_t = S.now("q_gw")
                    for j in range(4):
                        for kk in range(16):
                            k = kh * 16 + kk
                            w = []
                            if kk == 0:
                                w = [("q_gw", wl_t), ("cv", S.now("cv"))] + list(hq_done)
                            E("tensor", lambda e, kk=kk, k=k, j=j, n=n: e.matmul(
                                p_dn[j][:, :], hqT[k][:, j * 128:(j + 1) * 128],
                                wdl[kk][:, :], start=(k == 0), stop=(k == KI - 1)),
                              waits=w, incs=([("ct", 1)] if kk == 15 else ()))
                    if kh == 1:
                        for j in range(4):
                            td = S.now("ct") - (3 - j)
                            down_done[(j, n)] = E(
                                "vector", lambda e, j=j, n=n: e.scalar_tensor_tensor(
                                    outt[j][:, n * TC:(n + 1) * TC], p_dn[j][:, :],
                                    dqd[:, j:j + 1], xln[j][:, n * TC:(n + 1) * TC],
                                    OP.mult, OP.add),
                                waits=[("ct", td)], incs=[("cv", 1)])["cv"]

            for j in range(4):
                lnd = layernorm(outt[j], j, down_done[(j, 1)])
                final_dma = dma("sync", "q_y",
                                y_out[i * TC + j * 128: i * TC + (j + 1) * 128, :],
                                outt[j][:, :], waits=[("cv", lnd)])

        E("sync", None, waits=[("q_y", S.now("q_y")), ("q_tap", S.now("q_tap"))])

        semh = {k: sems[v] for k, v in SEMMAP.items()}
        with nc.Block() as block:
            def make(engname):
                def runner(eng):
                    for fn, waits, incs in S.ops[engname]:
                        for s, v in waits:
                            eng.wait_ge(semh[s], v)
                        if fn is None:
                            continue
                        ins = fn(eng)
                        for s, d in incs:
                            ins = ins.then_inc(semh[s], d)
                return runner
            block.sync(make("sync"))
            block.scalar(make("scalar"))
            block.vector(make("vector"))
            block.tensor(make("tensor"))
            block.gpsimd(make("gpsimd"))
        ffn.close()

    return nc


def make_inputs(inp):
    import ml_dtypes
    x = np.ascontiguousarray(np.asarray(inp["L0_attn_x"], dtype=np.float32))
    win_full = np.asarray(inp["L0_attn_in_proj_w"], dtype=np.float32)
    b_in_full = np.asarray(inp["L0_attn_in_proj_b"], dtype=np.float32)
    convw_f = np.asarray(inp["L0_attn_conv_w"], dtype=np.float32).reshape(DI, 4)
    convb_f = np.asarray(inp["L0_attn_conv_b"], dtype=np.float32)
    wxp_full = np.asarray(inp["L0_attn_x_proj_w"], dtype=np.float32)
    bxp_full = np.asarray(inp["L0_attn_x_proj_b"], dtype=np.float32)
    dtw_full = np.asarray(inp["L0_attn_dt_proj_w"], dtype=np.float32)
    dtb_full = np.asarray(inp["L0_attn_dt_proj_b"], dtype=np.float32)
    dvec_full = np.asarray(inp["L0_attn_D"], dtype=np.float32)
    wout_full = np.asarray(inp["L0_attn_out_proj_w"], dtype=np.float32)
    bout_full = np.asarray(inp["L0_attn_out_proj_b"], dtype=np.float32)
    wg_f = np.ascontiguousarray(np.asarray(inp["L0_ffn_w_gate"], dtype=np.float32))
    wu_f = np.ascontiguousarray(np.asarray(inp["L0_ffn_w_up"], dtype=np.float32))
    wd_f = np.ascontiguousarray(np.asarray(inp["L0_ffn_w_down"], dtype=np.float32))

    consts = np.zeros((128, NCONST), np.float32)
    consts[:, C_ONE] = 1.0
    for s in range(16):
        consts[:, C_NEGS + s] = -(s + 1)
    consts[:, C_EPS] = 1e-5
    consts[:, C_NEPS] = -1e-5
    onesbf = np.ones((1, 128), ml_dtypes.bfloat16)
    selbf = np.zeros((16, 16 * 128), ml_dtypes.bfloat16)
    for s_ in range(16):
        selbf[s_, s_ * 128:(s_ + 1) * 128] = 1
    onesf = np.ones((128, 1), np.float32)
    onesfr = np.ones((1, 128), np.float32)
    ident = np.eye(128).astype(ml_dtypes.bfloat16)

    wxp_cols = np.concatenate([
        wxp_full[:, 0:32], wxp_full[:, 32:32 + DH],
        wxp_full[:, 0:32], wxp_full[:, 32 + DH:32 + DI]], axis=1)

    maps = []
    for core in range(8):
        b, p = core >> 1, core & 1
        dh = slice(p * DH, (p + 1) * DH)
        win_my = np.concatenate(
            [win_full[:, dh], win_full[:, DI + p * DH: DI + (p + 1) * DH]], axis=1)
        b_in_my = np.concatenate(
            [b_in_full[dh], b_in_full[DI + p * DH: DI + (p + 1) * DH]])
        maps.append({
            "xfull": np.ascontiguousarray(x[b]),
            "xmy": np.ascontiguousarray(x[b, p * TOK:(p + 1) * TOK]),
            "win": np.ascontiguousarray(win_my),
            "b_in": np.ascontiguousarray(b_in_my),
            "wxp": np.ascontiguousarray(wxp_cols[dh, :]),
            "bxp_u": np.ascontiguousarray(bxp_full[32 + p * DH: 32 + (p + 1) * DH]),
            "bxp_ds": np.ascontiguousarray(bxp_full[0:16]),
            "bxp_b": np.ascontiguousarray(bxp_full[16:32]),
            "dtw": np.ascontiguousarray(dtw_full[:, dh]),
            "dtb": np.ascontiguousarray(dtb_full[dh]),
            "convw": np.ascontiguousarray(convw_f[dh, :]),
            "convb": np.ascontiguousarray(convb_f[dh]),
            "dvec": np.ascontiguousarray(dvec_full[dh]),
            "wout": np.ascontiguousarray(wout_full[dh, :]),
            "bout": np.ascontiguousarray(bout_full),
            "wg": wg_f, "wu": wu_f, "wd": wd_f,
            "consts": consts, "onesbf": onesbf, "selbf": selbf, "onesf": onesf,
            "onesfr": onesfr, "ident": ident,
        })
    return maps


_NC_CACHE = {}


def kernel(**inputs):
    if "main" not in _NC_CACHE:
        _NC_CACHE["main"] = build_nc()
    nc = _NC_CACHE["main"]
    maps = make_inputs(inputs)
    res = run_bass_kernel_spmd(nc, maps, core_ids=list(range(8)))
    out = np.empty((B, T, DM), np.float32)
    for core in range(8):
        b, p = core >> 1, core & 1
        out[b, p * TOK:(p + 1) * TOK, :] = res.results[core]["y"]
    return out


# version 15 (final)
# speedup vs baseline: 1.0236x; 1.0236x over previous
import numpy as np
from contextlib import ExitStack

import concourse.bass as bass
import concourse.mybir as mybir
from concourse.bass_utils import run_bass_kernel_spmd

F32 = mybir.dt.float32
BF16 = mybir.dt.bfloat16
I32 = mybir.dt.int32
AF = mybir.ActivationFunctionType
OP = mybir.AluOpType
AX = mybir.AxisListType

B, T, DM = 4, 4096, 1024
DI, DH, NS = 2048, 1024, 16
TC, NCH = 512, 8
TOK = 2048
NFC = TOK // TC
INNER = 4096
GROUPS = [[0, 1], [2, 3], [4, 5], [6, 7]]
KD = DH // 128
KM = DM // 128
KI = INNER // 128

C_ONE, C_NEGS, C_EPS, C_NEPS = 0, 1, 17, 18
NCONST = 24

DEBUG_TAPS = []


class Sched:
    ENGS = ("sync", "scalar", "vector", "tensor", "gpsimd")

    def __init__(self):
        self.ops = {e: [] for e in self.ENGS}
        self.cnt = {}

    def emit(self, eng, fn, waits=(), incs=()):
        rec = []
        for s, d in incs:
            self.cnt[s] = self.cnt.get(s, 0) + d
            rec.append((s, d))
        self.ops[eng].append((fn, list(waits), rec))
        return {s: self.cnt[s] for s, _ in rec}

    def now(self, sem):
        return self.cnt.get(sem, 0)


def build_nc(taps=()):
    taps = dict(taps)
    nc = bass.Bass("TRN2", target_bir_lowering=False, debug=False)

    def din(name, shape, dtype=F32):
        return nc.dram_tensor(name, shape, dtype, kind="ExternalInput").ap()

    def dscr(name, shape, dtype=F32):
        return nc.dram_tensor(name, shape, dtype).ap()

    xfull = din("xfull", [T, DM])
    xmy = din("xmy", [TOK, DM])
    win = din("win", [DM, DI])
    b_in = din("b_in", [DI])
    wxp = din("wxp", [DH, 2112])
    bxp_u = din("bxp_u", [DH])
    bxp_ds = din("bxp_ds", [16])
    bxp_b = din("bxp_b", [16])
    dtw = din("dtw", [NS, DH])
    dtb = din("dtb", [DH])
    convw = din("convw", [DH, 4])
    convb = din("convb", [DH])
    dvec = din("dvec", [DH])
    wout = din("wout", [DH, DM])
    bout = din("bout", [DM])
    wg = din("wg", [INNER, DM])
    wu = din("wu", [INNER, DM])
    wd = din("wd", [DM, INNER])
    consts = din("consts", [128, NCONST])
    onesbf = din("onesbf", [1, 128], BF16)
    selbf = din("selbf", [16, 16 * 128], BF16)
    onesf = din("onesf", [128, 1])
    onesfr = din("onesfr", [1, 128])
    ident = din("ident", [128, 128], BF16)

    cc1_in = [dscr(f"cc1i{c}", [2112, TC]) for c in range(NCH)]
    cc1_out = [dscr(f"cc1o{c}", [1056, TC]) for c in range(NCH)]
    cc2_in = [dscr(f"cc2i{c}", [2 * TC, DM]) for c in range(NFC)]
    cc2_out = [dscr(f"cc2o{c}", [TC, DM]) for c in range(NFC)]
    wgT_d = dscr("wgT_d", [KM * 32 * 128, 128], BF16)
    scr_w = dscr("scr_w", [128, 1])
    scr_m = dscr("scr_m", [1, 2])
    wuT_d = dscr("wuT_d", [KM * 32 * 128, 128], BF16)
    wdT_d = dscr("wdT_d", [KI * 8 * 128, 128], BF16)

    y_out = nc.dram_tensor("y", [TOK, DM], F32, kind="ExternalOutput").ap()
    tap_out = {}
    for tname, shape in taps.items():
        tap_out[tname] = nc.dram_tensor("tap_" + tname, shape, F32,
                                        kind="ExternalOutput").ap()

    S = Sched()
    E = S.emit

    outer = ExitStack()

    def sbo(name, shape, dtype=F32):
        return outer.enter_context(nc.sbuf_tensor(name, shape, dtype))

    with outer:
        cc = sbo("cc", [128, NCONST])
        ones_bf = sbo("ones_bf", [1, 128], BF16)
        sel_bf = sbo("sel_bf", [16, 16 * 128], BF16)
        ones_f = sbo("ones_f", [128, 1])
        ones_fr = sbo("ones_fr", [1, 128])
        idn = sbo("idn", [128, 128], BF16)
        binp = sbo("binp", [128, 16])
        bxpu = sbo("bxpu", [128, KD])
        bxpds = sbo("bxpds", [16, 1])
        bxpb = sbo("bxpb", [16, 1])
        dtb_t = sbo("dtb_t", [128, KD])
        cwt = sbo("cwt", [128, 4 * KD])
        cbt = sbo("cbt", [128, KD])
        dvt = sbo("dvt", [128, KD])
        boutb = sbo("boutb", [128, DM])
        boutrow = sbo("boutrow", [1, DM], BF16)
        carry = sbo("carry", [128, KD * NS], BF16)
        mclipc = sbo("mclipc", [128, 4])
        col = sbo("col", [128, 18])
        wstg = [sbo(f"wstg{i}", [128, 1056]) for i in range(2)]
        pst = ExitStack()
        p_tr = pst.enter_context(nc.psum_tensor("p_tr", [128, 512], BF16))
        p_mm = [pst.enter_context(nc.psum_tensor(f"p_mm{i}", [128, 512], F32)) for i in range(2)]
        p_bc = [pst.enter_context(nc.psum_tensor(f"p_bc{i}", [128, 512], F32)) for i in range(2)]
        p_op = [pst.enter_context(nc.psum_tensor(f"p_op{i}", [128, 512], F32)) for i in range(2)]
        outer.enter_context(pst.pop_all())

        SEMNAMES = ["cs", "cv_", "ct", "cc1", "cc2", "q_const", "q_wl0", "q_wl1",
                    "q_x", "q_s0", "q_s1", "q_s2", "q_s3", "q_u", "q_gw", "q_xt",
                    "q_h0", "q_h1", "q_wq", "q_t0", "q_t1", "q_f0", "q_y", "q_tap"]
        sems = {}
        for sname in SEMNAMES:
            sems[sname] = outer.enter_context(nc.semaphore(sname))
        SEMMAP = {("cv" if n == "cv_" else n): n for n in SEMNAMES}

        def dma(queue, sem, dst, src, waits=(), slow=False):
            return E(queue, lambda e, d=dst, s=src, sl=slow: e.dma_start(
                out=d, in_=s, allow_slow_non_contiguous=sl),
                waits=waits, incs=[(sem, 16)])[sem]

        def dmat(queue, sem, dst, src, waits=()):
            return E(queue, lambda e, d=dst, s=src: e.dma_start_transpose(d, s),
                     waits=waits, incs=[(sem, 16)])[sem]

        def tap(name, src, waits, queue="sync"):
            if name in tap_out:
                d = dma(queue, "q_tap", tap_out[name][:, :], src, waits=waits)
                for eng in ("vector", "scalar", "tensor"):
                    E(eng, None, waits=[("q_tap", d)])

        def ap1(t, n, m):
            return bass.AP(t.tensor, 0, [[1, n], [n, m]])

        dma("sync", "q_const", cc[:, :], consts[:, :])
        dma("sync", "q_const", ones_bf[:, :], onesbf[:, :])
        dma("sync", "q_const", sel_bf[:, :], selbf[:, :])
        dma("sync", "q_const", ones_f[:, :], onesf[:, :])
        dma("sync", "q_const", ones_fr[:, :], onesfr[:, :])
        dma("sync", "q_const", idn[:, :], ident[:, :])
        dma("sync", "q_const", binp[:, :], ap1(b_in, 128, 16), slow=True)
        dma("sync", "q_const", bxpu[:, :], ap1(bxp_u, 128, KD), slow=True)
        dma("sync", "q_const", bxpds[:, :], bass.AP(bxp_ds.tensor, 0, [[1, 16], [16, 1]]), slow=True)
        dma("sync", "q_const", bxpb[:, :], bass.AP(bxp_b.tensor, 0, [[1, 16], [16, 1]]), slow=True)
        dma("sync", "q_const", dtb_t[:, :], ap1(dtb, 128, KD), slow=True)
        dma("sync", "q_const", cwt[:, :], bass.AP(convw.tensor, 0, [[4, 128], [512, KD], [1, 4]]), slow=True)
        dma("sync", "q_const", cbt[:, :], ap1(convb, 128, KD), slow=True)
        dma("sync", "q_const", dvt[:, :], ap1(dvec, 128, KD), slow=True)
        ld0 = dma("sync", "q_const", wstg[0][0:1, 0:DM], bass.AP(bout.tensor, 0, [[DM, 1], [1, DM]]))
        tap("cwt", cwt[:, :], [("q_const", S.now("q_const"))])
        tap("cbt", cbt[:, :], [("q_const", S.now("q_const"))])

        w_brow = E("scalar", lambda e: e.copy(boutrow[:, :], wstg[0][0:1, 0:DM]),
                   waits=[("q_const", ld0)], incs=[("cs", 1)])["cs"]
        tb = E("tensor", lambda e: e.matmul(p_mm[0][:, :], ones_bf[:, :],
                                            boutrow[0:1, 0:TC], start=True, stop=True),
               waits=[("cs", w_brow)], incs=[("ct", 1)])["ct"]
        tb2 = E("tensor", lambda e: e.matmul(p_mm[1][:, :], ones_bf[:, :],
                                             boutrow[0:1, TC:DM], start=True, stop=True),
                incs=[("ct", 1)])["ct"]
        E("scalar", lambda e: e.copy(boutb[:, 0:TC], p_mm[0][:, :]),
          waits=[("ct", tb)], incs=[("cs", 1)])
        evb = E("scalar", lambda e: e.copy(boutb[:, TC:DM], p_mm[1][:, :]),
                waits=[("ct", tb2)], incs=[("cs", 1)])["cs"]

        mix = ExitStack()

        def sbm(name, shape, dtype=F32):
            return mix.enter_context(nc.sbuf_tensor(name, shape, dtype))

        win_b = [sbm(f"win_b{k}", [128, DI], BF16) for k in range(KM)]
        wxp_b = [sbm(f"wxp_b{k}", [128, 2112], BF16) for k in range(KD)]
        wout_b = [sbm(f"wout_b{k}", [128, DM], BF16) for k in range(KD)]
        dtw_b = sbm("dtw_b", [16, DH], BF16)
        xtm = [sbm("xtm0", [128, DM], BF16)]
        xfm = [sbm(f"xfm{k}", [128, TC], BF16) for k in range(KM)]
        xc = [sbm(f"xc{m}", [128, TC + 3], BF16) for m in range(KD)]
        cv_t = sbm("cv_t", [128, TC], BF16)
        zs = [sbm(f"zs{m}", [128, TC], BF16) for m in range(KD)]
        ut = [sbm(f"ut{m}", [128, TC], BF16) for m in range(KD)]
        dl = [sbm("dl", [128, TC])]
        dlb = [sbm("dlb", [128, TC], BF16)]
        udt = dlb[0]
        dbfr = sbm("dbfr", [48, TC])
        dsb_bf = sbm("dsb_bf", [16, TC], BF16)
        bbf = sbm("bbf", [16, TC], BF16)
        bbc = [sbm(f"bbc{s}", [128, TC], BF16) for s in range(NS)]
        relu_t = sbm("relu_t", [128, TC])
        abs_t = sbm("abs_t", [128, TC])
        exp_t = sbm("exp_t", [128, TC])
        ln_t = abs_t
        vbf = sbm("vbf", [128, TC], BF16)
        a_rot = [sbm(f"a_rot{i}", [128, TC]) for i in range(2)]
        b_rot = [sbm("b_rot", [128, TC])]
        h_all = sbm("h_all", [128, NS * TC], BF16)
        f1 = sbm("f1", [128, NS * TC // 2], BF16)
        yb = [sbm(f"yb{m}", [128, TC], BF16) for m in range(KD)]
        f2 = h_all[:, 0:NS * TC // 4]
        f3 = f1[:, 0:NS * TC // 8]
        hsum = f1[:, NS * TC // 8: NS * TC // 8 + TC]
        xcs = xfm

        def load_weights(dst_tiles, src, rows, cols, startw):
            evs = [startw]
            pieces = [(o, min(1056, cols - o)) for o in range(0, cols, 1056)]
            i = 0
            for k in range(rows // 128):
                for (o, wdt) in pieces:
                    w = [("cs", evs[-2])] if len(evs) >= 2 else [("cs", evs[-1])]
                    d = dma("sync", f"q_wl{i % 2}", wstg[i % 2][:, 0:wdt],
                            src[k * 128:(k + 1) * 128, o:o + wdt], waits=w)
                    evs.append(E("scalar", lambda e, k=k, o=o, wdt=wdt, i=i: e.copy(
                        dst_tiles[k][:, o:o + wdt], wstg[i % 2][:, 0:wdt]),
                        waits=[(f"q_wl{i % 2}", d)], incs=[("cs", 1)])["cs"])
                    i += 1
            return evs[-1]

        lw = load_weights(win_b, win, DM, DI, evb)
        lw = load_weights(wxp_b, wxp, DH, 2112, lw)
        lw = load_weights(wout_b, wout, DH, DM, lw)
        d_dtw = dma("sync", "q_wl0", wstg[0][0:16, 0:DH], dtw[:, :], waits=[("cs", lw)])
        WREADY = E("scalar", lambda e: e.copy(dtw_b[:, :], wstg[0][0:16, 0:DH]),
                   waits=[("q_wl0", d_dtw)], incs=[("cs", 1)])["cs"]

        last_store = {}
        rs2_ready = {}
        scan_prev = {}
        vv_guard = {}
        udt_guard = [None]
        prev_ag = [None]
        prev_sil = {}
        prev_dl = None
        slot_last = [None, None, None, None]

        for c in range(NCH):
            base = c * TC
            tr_ev = []
            for j in range(4):
                w = [("cs", WREADY)] if c == 0 and j == 0 else []
                if j >= 1:
                    w.append(("cs", tr_ev[-1]))
                if c > 0 and j == 0:
                    w.append(("cv", S.now("cv")))
                xl = dma("gpsimd", "q_x", xtm[0][:, :],
                         xfull[base + j * 128: base + (j + 1) * 128, :], waits=w)
                for f in range(KM):
                    wt = [("q_x", xl)]
                    if tr_ev:
                        wt.append(("cs", tr_ev[-1]))
                    tt = E("tensor", lambda e, f=f: e.transpose(
                        p_tr[:, 0:128], xtm[0][:, f * 128:(f + 1) * 128], idn[:, :]),
                        waits=wt, incs=[("ct", 1)])["ct"]
                    tr_ev.append(E("scalar", lambda e, j=j, f=f: e.copy(
                        xfm[f][:, j * 128:(j + 1) * 128], p_tr[:, 0:128]),
                        waits=[("ct", tt)], incs=[("cs", 1)])["cs"])
            xfm_done = tr_ev[-1]

            halo = []
            for m in range(KD):
                if c == 0:
                    halo.append(E("vector", lambda e, m=m: e.memset(xc[m][:, 0:3], 0.0),
                                  incs=[("cv", 1)])["cv"])
                else:
                    halo.append(E("vector", lambda e, m=m: e.tensor_copy(
                        xc[m][:, 0:3], xc[m][:, TC:TC + 3]),
                        waits=[("cs", prev_sil[m])], incs=[("cv", 1)])["cv"])

            ev_mm = {}
            for m in range(16):
                bank = m % 2
                for k in range(KM):
                    w = []
                    if k == 0:
                        w = [("cs", xfm_done)]
                        if m >= 2:
                            w.append(("cs", ev_mm[m - 2]))
                    E("tensor", lambda e, k=k, m=m, bank=bank: e.matmul(
                        p_mm[bank][:, :], win_b[k][:, m * 128:(m + 1) * 128],
                        xfm[k][:, :], start=(k == 0), stop=(k == KM - 1)),
                      waits=w, incs=([("ct", 1)] if k == KM - 1 else ()))
                mm_t = S.now("ct")
                if m < KD:
                    ev_mm[m] = E("scalar", lambda e, m=m, bank=bank: e.activation(
                        xc[m][:, 3:TC + 3], p_mm[bank][:, :], AF.Identity,
                        bias=binp[:, m:m + 1]),
                        waits=[("ct", mm_t), ("cv", halo[m])], incs=[("cs", 1)])["cs"]
                else:
                    ev_mm[m] = E("scalar", lambda e, m=m, bank=bank: e.activation(
                        zs[m - KD][:, :], p_mm[bank][:, :], AF.Silu,
                        bias=binp[:, m:m + 1]),
                        waits=[("ct", mm_t)], incs=[("cs", 1)])["cs"]

            if c == 0:
                tap("xc0", xc[0][:, 3:TC + 3], [("cs", ev_mm[0])], queue="gpsimd")
                tap("z0", zs[0][:, :], [("cs", ev_mm[KD])], queue="gpsimd")
                tap("xfm0", xfm[0][:, :], [("cs", xfm_done)], queue="gpsimd")
            sil = {}
            for m in range(KD):
                wv1 = [("cs", ev_mm[15])]
                if m > 0:
                    wv1.append(("cs", sil[m - 1]))
                v1 = E("vector", lambda e, m=m: e.tensor_scalar(
                    cv_t[:, :], xc[m][:, 0:TC], cwt[:, 4 * m:4 * m + 1], None, OP.mult),
                    waits=wv1, incs=[("cv", 1)])["cv"]
                for kk in range(1, 4):
                    v1 = E("vector", lambda e, m=m, kk=kk: e.scalar_tensor_tensor(
                        cv_t[:, :], xc[m][:, kk:TC + kk], cwt[:, 4 * m + kk:4 * m + kk + 1],
                        cv_t[:, :], OP.mult, OP.add), incs=[("cv", 1)])["cv"]
                if c == 0 and m == 0:
                    tap("cv0", cv_t[:, :], [("cv", v1)], queue="gpsimd")
                    tap("xch0", xc[0][:, :], [("cv", v1)], queue="gpsimd")
                sil[m] = E("scalar", lambda e, m=m: e.activation(
                    xcs[m][:, :], cv_t[:, :], AF.Silu, bias=cbt[:, m:m + 1]),
                    waits=[("cv", v1)], incs=[("cs", 1)])["cs"]
            prev_sil = sil

            if c == 0:
                tap("xcs0", xcs[0][:, :], [("cs", sil[0])], queue="gpsimd")
            mcols = [(0, 32)] + [(32 + i * 128, 128) for i in range(8)] + \
                    [(1056, 32)] + [(1088 + i * 128, 128) for i in range(8)]
            stg_dma = []
            stg_ev = []
            for mi, (c0, width) in enumerate(mcols):
                bank = mi % 2
                for k in range(KD):
                    w = []
                    if k == 0:
                        w = [("cs", sil[KD - 1])]
                        if mi >= 2:
                            w.append(("cs", stg_ev[mi - 2]))
                    E("tensor", lambda e, k=k, c0=c0, width=width, bank=bank: e.matmul(
                        p_mm[bank][0:width, :], wxp_b[k][:, c0:c0 + width],
                        xcs[k][:, :], start=(k == 0), stop=(k == KD - 1)),
                      waits=w, incs=([("ct", 1)] if k == KD - 1 else ()))
                mm_t = S.now("ct")
                sg = mi % 4
                wv = [("ct", mm_t)]
                if slot_last[sg] is not None:
                    wv.append((f"q_s{sg}", slot_last[sg]))
                ev = E("scalar", lambda e, width=width, sg=sg, bank=bank: e.copy(
                    wstg[sg % 2][0:width, (sg // 2) * TC:(sg // 2 + 1) * TC],
                    p_mm[bank][0:width, :]), waits=wv, incs=[("cs", 1)])["cs"]
                stg_ev.append(ev)
                slot_last[sg] = dma("sync", f"q_s{sg}", cc1_in[c][c0:c0 + width, :],
                                    wstg[sg % 2][0:width, (sg // 2) * TC:(sg // 2 + 1) * TC],
                                    waits=[("cs", ev)])
                stg_dma.append(slot_last[sg])

            rs1 = E("gpsimd", lambda e, c=c: e.collective_compute(
                "ReduceScatter", OP.add, replica_groups=GROUPS,
                ins=[cc1_in[c][:, :]], outs=[cc1_out[c][:, :]]),
                waits=[(f"q_s{i}", S.now(f"q_s{i}")) for i in range(4)],
                incs=[("cc1", 1)])["cc1"]
            d_dsb = dma("gpsimd", "q_u", dbfr[0:16, :], cc1_out[c][0:16, :],
                        waits=[("cc1", rs1)])
            d_bfr = dma("gpsimd", "q_u", dbfr[32:48, :], cc1_out[c][16:32, :])
            d_u = []
            for i in range(KD):
                w = [("cv", S.now("cv"))] if (c > 0 and i == 0) else []
                d_u.append(dma("gpsimd", "q_u", ut[i][:, :],
                               cc1_out[c][32 + i * 128:32 + (i + 1) * 128, :], waits=w))
            u_all = S.now("q_u")
            bias_u = []
            for i in range(KD):
                bias_u.append(E("vector", lambda e, i=i: e.tensor_scalar(
                    ut[i][:, :], ut[i][:, :], bxpu[:, i:i + 1], None, OP.add),
                    waits=[("q_u", u_all)], incs=[("cv", 1)])["cv"])
            bias_db = E("vector", lambda e: e.tensor_scalar(
                dbfr[0:16, :], dbfr[0:16, :], bxpds[:, 0:1], None, OP.add),
                waits=[("q_u", u_all)], incs=[("cv", 1)])["cv"]
            bias_b = E("vector", lambda e: e.tensor_scalar(
                dbfr[32:48, :], dbfr[32:48, :], bxpb[:, 0:1], None, OP.add),
                waits=[("q_u", u_all)], incs=[("cv", 1)])["cv"]
            c_dsbf = E("scalar", lambda e: e.copy(dsb_bf[:, :], dbfr[0:16, :]),
                       waits=[("cv", bias_db)], incs=[("cs", 1)])["cs"]
            c_bbf = E("scalar", lambda e: e.copy(bbf[:, :], dbfr[32:48, :]),
                      waits=[("cv", bias_b)], incs=[("cs", 1)])["cs"]

            if c == 0:
                tap("ds0", dbfr[0:16, :], [("cv", bias_db)])
                tap("b0", dbfr[32:48, :], [("cv", bias_b)])
                tap("u0", ut[0][:, :], [("cv", bias_u[0])], queue="gpsimd")
            bb_ev = {}
            for s in range(NS):
                bank = s % 2
                w = [("cs", c_bbf)]
                if s >= 2:
                    w.append(("cs", bb_ev[s - 2]))
                if c > 0 and s < 2:
                    w.append(("cv", scan_prev.get("bbc", 0)))
                tt = E("tensor", lambda e, s=s, bank=bank: e.matmul(
                    p_bc[bank][:, :], sel_bf[:, s * 128:(s + 1) * 128], bbf[:, :],
                    start=True, stop=True), waits=w, incs=[("ct", 1)])["ct"]
                bb_ev[s] = E("scalar", lambda e, s=s, bank=bank: e.copy(
                    bbc[s][:, :], p_bc[bank][:, :]),
                    waits=[("ct", tt)], incs=[("cs", 1)])["cs"]

            y_done = {}
            dl_ev = {}
            for k in range(KD):
                bank = k % 2
                w = [("cs", c_dsbf)]
                if k >= 2:
                    w.append(("cv", dl_ev[k - 2]))
                tt = E("tensor", lambda e, k=k, bank=bank: e.matmul(
                    p_mm[bank][:, :], dtw_b[:, k * 128:(k + 1) * 128], dsb_bf[:, :],
                    start=True, stop=True), waits=w, incs=[("ct", 1)])["ct"]
                wr1 = [("ct", tt)]
                if k > 0:
                    wr1.append(("cv", dl_ev[k - 1]))
                elif prev_dl is not None:
                    wr1.append(("cv", prev_dl))
                r1 = E("scalar", lambda e, bank=bank, k=k: e.activation(
                    relu_t[:, :], p_mm[bank][:, :], AF.Relu, bias=dtb_t[:, k:k + 1]),
                    waits=wr1, incs=[("cs", 1)])["cs"]
                E("scalar", lambda e, bank=bank, k=k: e.activation(
                    abs_t[:, :], p_mm[bank][:, :], AF.Abs, bias=dtb_t[:, k:k + 1]),
                  incs=[("cs", 1)])
                E("scalar", lambda e: e.activation(
                    exp_t[:, :], abs_t[:, :], AF.Exp, scale=-1.0), incs=[("cs", 1)])
                r4 = E("scalar", lambda e: e.activation(
                    ln_t[:, :], exp_t[:, :], AF.Ln, bias=cc[:, C_ONE:C_ONE + 1]),
                    incs=[("cs", 1)])["cs"]
                dl_ev[k] = E("vector", lambda e, k=k: e.tensor_tensor(
                    dl[0][:, :], relu_t[:, :], ln_t[:, :], OP.add),
                    waits=[("cs", r4)] + ([("cs", prev_ag[0])] if prev_ag[0] is not None else []), incs=[("cv", 1)])["cv"]
                prev_dl = dl_ev[k]
                if c == 0 and k == 0:
                    tap("dl0", dl[0][:, :], [("cv", dl_ev[0])], queue="gpsimd")
                wdlb = [("cv", dl_ev[k])]
                if vv_guard.get(0) is not None:
                    wdlb.append(("cv", vv_guard[0]))
                if udt_guard[0] is not None:
                    wdlb.append(("cv", udt_guard[0]))
                dlb_ev = E("scalar", lambda e, k=k: e.copy(
                    dlb[0][:, :], dl[0][:, :]),
                    waits=wdlb, incs=[("cs", 1)])["cs"]
                vv = E("vector", lambda e, k=k: e.tensor_tensor(
                    vbf[:, :], dlb[0][:, :], ut[k][:, :], OP.mult),
                    waits=[("cv", bias_u[k]), ("cs", dlb_ev),
                           ("cs", bb_ev[NS - 1])], incs=[("cv", 1)])["cv"]
                vv_guard[0] = vv
                ag = []
                sc_last = None
                for s in range(NS):
                    w = []
                    if s == 0:
                        w.append(("cv", dl_ev[k]))
                    if (s % 2) in scan_prev:
                        w.append(("cv", scan_prev[s % 2]))
                    ag.append(E("scalar", lambda e, k=k, s=s: e.activation(
                        a_rot[s % 2][:, :], dl[0][:, :], AF.Exp,
                        scale=cc[:, C_NEGS + s:C_NEGS + s + 1]),
                        waits=w, incs=[("cs", 1)])["cs"])
                    bE = E("vector", lambda e, s=s: e.tensor_tensor(
                        b_rot[0][:, :], vbf[:, :], bbc[s][:, :], OP.mult),
                        waits=[("cv", vv)] if s == 0 else [], incs=[("cv", 1)])["cv"]
                    if c == 0:
                        init = 0.0
                    else:
                        init = carry[:, k * NS + s:k * NS + s + 1]
                    sc = E("vector", lambda e, s=s, init=init: e.tensor_tensor_scan(
                        h_all[:, s * TC:(s + 1) * TC], a_rot[s % 2][:, :],
                        b_rot[0][:, :], init, OP.mult, OP.add),
                        waits=[("cs", ag[s])], incs=[("cv", 1)])["cv"]
                    scan_prev[s % 2] = sc
                    sc_last = sc
                prev_ag[0] = ag[NS - 1]
                scan_prev["bbc"] = sc_last
                if c == 0 and k == 0:
                    tap("h0", h_all[:, 0:TC], [("cv", sc_last)], queue="gpsimd")
                    tap("h15", h_all[:, 15 * TC:16 * TC], [("cv", sc_last)], queue="gpsimd")
                    tap("a0", a_rot[0][:, :], [("cs", ag[0])], queue="gpsimd")
                    tap("b_0", b_rot[0][:, :], [("cv", sc_last)], queue="gpsimd")
                hre = h_all[:, :].rearrange("p (s t) -> p s t", s=NS)
                cre = carry[:, k * NS:(k + 1) * NS].rearrange("p (s o) -> p s o", o=1)
                cs_ev = E("scalar", lambda e, hre=hre, cre=cre: e.copy(
                    cre, hre[:, :, TC - 1:TC]),
                    waits=[("cv", sc_last)], incs=[("cs", 1)])["cs"]
                E("vector", lambda e: e.tensor_tensor(
                    f1[:, :], h_all[:, 0:NS * TC // 2], h_all[:, NS * TC // 2:NS * TC],
                    OP.add), waits=[("cs", cs_ev)], incs=[("cv", 1)])
                E("vector", lambda e: e.tensor_tensor(
                    f2, f1[:, 0:NS * TC // 4], f1[:, NS * TC // 4:NS * TC // 2],
                    OP.add), incs=[("cv", 1)])
                E("vector", lambda e: e.tensor_tensor(
                    f3, f2[:, 0:NS * TC // 8], f2[:, NS * TC // 8:NS * TC // 4],
                    OP.add), incs=[("cv", 1)])
                E("vector", lambda e: e.tensor_tensor(
                    hsum, f3[:, 0:TC], f3[:, TC:2 * TC], OP.add), incs=[("cv", 1)])
                wud = [("cv", bias_u[k]), ("cv", vv_guard[0])]
                if udt_guard[0] is not None:
                    wud.append(("cv", udt_guard[0]))
                ud_ev = E("scalar", lambda e, k=k: e.activation(
                    udt[:, :], ut[k][:, :], AF.Identity, scale=dvt[:, k:k + 1]),
                    waits=wud, incs=[("cs", 1)])["cs"]
                udt_guard[0] = E("vector", lambda e: e.tensor_tensor(
                    hsum, hsum, udt[:, :], OP.add),
                    waits=[("cs", ud_ev)], incs=[("cv", 1)])["cv"]
                y_done[k] = E("vector", lambda e, k=k: e.tensor_tensor(
                    yb[k][:, :], hsum, zs[k][:, :], OP.mult), incs=[("cv", 1)])["cv"]

            if c == 0:
                tap("y0", yb[0][:, :], [("cv", y_done[0])], queue="gpsimd")
                tap("y7", yb[7][:, :], [("cv", y_done[7])], queue="gpsimd")
            op_ev = []
            for j in range(4):
                for n in range(2):
                    for k in range(KD):
                        w = [("cv", y_done[k])]
                        if k == 0 and len(op_ev) >= 2:
                            w.append(("cs", op_ev[-2]))
                        E("tensor", lambda e, j=j, n=n, k=k: e.matmul(
                            p_op[n][:, :], yb[k][:, j * 128:(j + 1) * 128],
                            wout_b[k][:, n * TC:(n + 1) * TC],
                            start=(k == 0), stop=(k == KD - 1)),
                          waits=w, incs=([("ct", 1)] if k == KD - 1 else ()))
                    mm_t = S.now("ct")
                    sg = (2 * j + n) % 4
                    wv = [("ct", mm_t), (f"q_s{sg}", slot_last[sg])]
                    ev = E("scalar", lambda e, n=n, sg=sg: e.copy(
                        wstg[sg % 2][:, (sg // 2) * TC:(sg // 2 + 1) * TC],
                        p_op[n][:, :]), waits=wv, incs=[("cs", 1)])["cs"]
                    op_ev.append(ev)
                    if c == 0 and j == 0 and n == 0:
                        tap("op00", wstg[0][:, 0:TC], [("cs", ev)])
                    half = 0 if c < 4 else 1
                    cp = c % 4
                    slot_last[sg] = dma("sync", f"q_s{sg}",
                                        cc2_in[cp][half * TC + j * 128: half * TC + (j + 1) * 128,
                                                   n * TC:(n + 1) * TC],
                                        wstg[sg % 2][:, (sg // 2) * TC:(sg // 2 + 1) * TC],
                                        waits=[("cs", ev)])
            if c == 0:
                tap("pin", cc2_in[0][0:128, :],
                    [(f"q_s{i}", S.now(f"q_s{i}")) for i in range(4)])
            if c == 4:
                tap("pin4", cc2_in[0][TC:TC + 128, :],
                    [(f"q_s{i}", S.now(f"q_s{i}")) for i in range(4)])
            if c >= 4:
                rs2_ready[c - 4] = E("gpsimd", lambda e, i=c - 4: e.collective_compute(
                    "ReduceScatter", OP.add, replica_groups=GROUPS,
                    ins=[cc2_in[i][:, :]], outs=[cc2_out[i][:, :]]),
                    waits=[(f"q_s{i}", S.now(f"q_s{i}")) for i in range(4)],
                    incs=[("cc2", 1)])["cc2"]
                if c == 4:
                    tap("mixraw", cc2_out[0][0:128, :], [("cc2", rs2_ready[0])])

        MIXDONE = S.now("cc2")
        mix.close()

        ffn = ExitStack()

        def sbf(name, shape, dtype=F32):
            return ffn.enter_context(nc.sbuf_tensor(name, shape, dtype))

        qscope = ExitStack()
        wq_t = qscope.enter_context(nc.sbuf_tensor("wq_t", [128, INNER], F32))
        wq_t2 = qscope.enter_context(nc.sbuf_tensor("wq_t2", [128, INNER], F32))
        wq_i = qscope.enter_context(nc.sbuf_tensor("wq_i", [128, INNER], I32))
        wq_bf = qscope.enter_context(nc.sbuf_tensor("wq_bf", [128, INNER], BF16))
        trp = [qscope.enter_context(nc.sbuf_tensor(f"trp{i}", [128, 128], BF16)) for i in range(2)]
        wsum = qscope.enter_context(nc.sbuf_tensor("wsum", [128, 32], F32))
        wcol = qscope.enter_context(nc.sbuf_tensor("wcol", [128, 4], F32))

        def quant_weight(src, rows, cols, dstT, mclip_idx, startw):
            nrt = rows // 128
            evs = []
            for r in range(nrt):
                t_ = wq_t if r % 2 == 0 else wq_t2
                sl = f"q_wl{r % 2}"
                w = [("cv", evs[r - 2])] if r >= 2 else list(startw)
                d = dma("sync", sl, t_[:, 0:cols], src[r * 128:(r + 1) * 128, :],
                        waits=w)
                evs.append(E("vector", lambda e, r=r, cols=cols, t_=t_: e.tensor_reduce(
                    wsum[:, r:r + 1], t_[:, 0:cols], AX.X, OP.add,
                    apply_absolute_value=True),
                    waits=[(sl, d)], incs=[("cv", 1)])["cv"])
            sb = E("scalar", lambda e, nrt=nrt: e.copy(
                wq_t2[:, 0:nrt], wsum[:, 0:nrt]),
                waits=[("cv", evs[-1])], incs=[("cs", 1)])["cs"]
            rsum = E("vector", lambda e, nrt=nrt: e.tensor_reduce(
                wcol[:, 0:1], wq_t2[:, 0:nrt], AX.X, OP.add),
                waits=[("cs", sb)], incs=[("cv", 1)])["cv"]
            if mclip_idx == 2:
                tap("wdsum", wsum[:, 0:8], [("cv", rsum)])
                tap("wdcol", wcol[:, 0:4], [("cv", rsum)])
            d1 = dma("sync", "q_wq", scr_w[:, :], wcol[:, 0:1], waits=[("cv", rsum)])
            d2 = dma("sync", "q_wq", wq_t2[0:1, 0:128],
                     bass.AP(scr_w.tensor, 0, [[128, 1], [1, 128]]),
                     waits=[("q_wq", d1)])
            m0 = E("vector", lambda e: e.tensor_reduce(
                wcol[0:1, 0:1], wq_t2[0:1, 0:128], AX.X, OP.add),
                waits=[("q_wq", d2)], incs=[("cv", 1)])["cv"]
            if mclip_idx == 2:
                tap("wdrow", wq_t2[0:128, 0:128], [("cv", m0)])
            d3 = dma("sync", "q_wq", scr_m[:, 0:1], wcol[0:1, 0:1], waits=[("cv", m0)])
            d4 = dma("sync", "q_wq", wcol[:, 0:1],
                     bass.AP(scr_m.tensor, 0, [[0, 128], [1, 1]]),
                     waits=[("q_wq", d3)])
            mm1 = E("vector", lambda e, rows=rows, cols=cols: e.tensor_scalar(
                mclipc[:, mclip_idx:mclip_idx + 1], wcol[:, 0:1],
                1.0 / (rows * cols), 1e-5, OP.mult, OP.max),
                waits=[("q_wq", d4)], incs=[("cv", 1)])["cv"]
            mm1b = E("scalar", lambda e: e.copy(
                wcol[:, 1:2], mclipc[:, mclip_idx:mclip_idx + 1]),
                waits=[("cv", mm1)], incs=[("cs", 1)])["cs"]
            eb2v = E("vector", lambda e: e.reciprocal(wcol[:, 3:4], wcol[:, 1:2]),
                     waits=[("cs", mm1b)], incs=[("cv", 1)])["cv"]
            eb2 = E("scalar", lambda e: e.copy(wcol[:, 2:3], wcol[:, 3:4]),
                    waits=[("cv", eb2v)], incs=[("cs", 1)])["cs"]
            prev_q2 = None
            tslot = [None, None]
            for r in range(nrt):
                w = [("cs", eb2)] if r == 0 else [("cv", prev_q2)]
                d = dma("sync", "q_wq", wq_t[:, 0:cols], src[r * 128:(r + 1) * 128, :],
                        waits=w)
                E("vector", lambda e, cols=cols: e.tensor_scalar(
                    wq_t2[:, 0:cols], wq_t[:, 0:cols], wcol[:, 2:3], None,
                    OP.mult), waits=[("q_wq", d)], incs=[("cv", 1)])
                q2 = E("vector", lambda e, cols=cols: e.tensor_scalar(
                    wq_t2[:, 0:cols], wq_t2[:, 0:cols], -1.0, 1.0,
                    OP.max, OP.min), incs=[("cv", 1)])["cv"]
                prev_q2 = q2
                wq3 = [("cv", q2)]
                for ts_ in (0, 1):
                    if tslot[ts_] is not None:
                        wq3.append((f"q_t{ts_}", tslot[ts_][0]))
                q3 = E("scalar", lambda e, cols=cols: e.activation(
                    wq_i[:, 0:cols], wq_t2[:, 0:cols], AF.Identity),
                    waits=wq3, incs=[("cs", 1)])["cs"]
                q4 = E("scalar", lambda e, cols=cols: e.copy(wq_bf[:, 0:cols],
                                                             wq_i[:, 0:cols]),
                       incs=[("cs", 1)])["cs"]
                for kb in range(cols // 128):
                    sl = kb % 2
                    wt = [("cs", q4)]
                    if tslot[sl] is not None:
                        wt.append((f"q_t{sl}", tslot[sl][0]))
                    tr = dmat("scalar", f"q_t{sl}", trp[sl][:, :],
                              wq_bf[:, kb * 128:(kb + 1) * 128], waits=wt)
                    out_d = dma("sync", f"q_t{sl}",
                                dstT[(kb * nrt + r) * 128:(kb * nrt + r + 1) * 128, :],
                                trp[sl][:, :], waits=[(f"q_t{sl}", tr)])
                    tslot[sl] = (out_d, tr)
            return (S.now("q_t0"), S.now("q_t1"))

        qs = quant_weight(wg, INNER, DM, wgT_d, 0, [("cc2", MIXDONE)])
        qs = quant_weight(wu, INNER, DM, wuT_d, 1,
                          [("q_t0", qs[0]), ("q_t1", qs[1])])
        qs = quant_weight(wd, DM, INNER, wdT_d, 2,
                          [("q_t0", qs[0]), ("q_t1", qs[1])])
        WQDONE = [("q_t0", qs[0]), ("q_t1", qs[1])]
        qscope.close()
        xres = [sbf("xres0", [128, DM])]
        xln = [sbf(f"xln{j}", [128, DM]) for j in range(4)]
        outt = [sbf(f"outt{j}", [128, DM]) for j in range(4)]
        mixt = outt
        hf = [sbf(f"hf{j}", [128, INNER], BF16) for j in range(4)]
        dump = sbf("dump", [128, DM])
        qint = sbf("qint", [128, TC], I32)
        xq_b = sbf("xq_b", [128, DM], BF16)
        xqT = [sbf(f"xqT{k}", [128, TC], BF16) for k in range(KM)]
        hq_b = [sbf(f"hq_b{i}", [128, TC], BF16) for i in range(2)]
        hqT = [sbf(f"hqT{k}", [128, TC], BF16) for k in range(KI)]
        wgl = [sbf(f"wgl{k}", [128, TC], BF16) for k in range(KM)]
        wul = [sbf(f"wul{k}", [128, TC], BF16) for k in range(KM)]
        wdl = [sbf(f"wdl{k}", [128, TC], BF16) for k in range(16)]
        dqg = sbf("dqg", [128, 4])
        dqu = sbf("dqu", [128, 4])
        dqd = sbf("dqd", [128, 4])
        relu2 = sbf("relu2", [128, TC])
        abs2 = sbf("abs2", [128, TC])


        lntapped = []

        def layernorm(eng_src_tile, j, wait_cv):
            s2 = E("scalar", lambda e: e.activation(
                dump[:, :], eng_src_tile[:, :], AF.Square),
                waits=[("cv", wait_cv)], incs=[("cs", 1)])["cs"]
            E("vector", lambda e: e.tensor_reduce(
                col[:, 0:1], eng_src_tile[:, :], AX.X, OP.add),
              waits=[("cv", wait_cv)], incs=[("cv", 1)])
            E("vector", lambda e: e.tensor_reduce(
                col[:, 1:2], dump[:, :], AX.X, OP.add),
              waits=[("cs", s2)], incs=[("cv", 1)])
            E("vector", lambda e: e.tensor_scalar(
                col[:, 2:3], col[:, 0:1], 1.0 / DM, None, OP.mult),
              incs=[("cv", 1)])
            E("vector", lambda e: e.tensor_scalar(
                col[:, 3:4], col[:, 1:2], 1.0 / DM, None, OP.mult), incs=[("cv", 1)])
            v4t = S.now("cv")
            E("scalar", lambda e: e.activation(
                col[:, 4:5], col[:, 2:3], AF.Square),
              waits=[("cv", v4t)], incs=[("cs", 1)])
            E("scalar", lambda e: e.activation(
                col[:, 5:6], col[:, 4:5], AF.Identity, scale=-1.0,
                bias=col[:, 3:4]), incs=[("cs", 1)])
            s3 = E("scalar", lambda e: e.activation(
                col[:, 6:7], col[:, 5:6], AF.Sqrt, bias=cc[:, C_EPS:C_EPS + 1]),
                incs=[("cs", 1)])["cs"]
            if not lntapped and "lncol" in tap_out:
                lntapped.append(1)
                tap("lncol", col[:, :], [("cs", s3)])
            E("vector", lambda e: e.reciprocal(col[:, 7:8], col[:, 6:7]),
              waits=[("cs", s3)], incs=[("cv", 1)])
            return E("vector", lambda e: e.tensor_scalar(
                eng_src_tile[:, :], eng_src_tile[:, :], col[:, 2:3], col[:, 7:8],
                OP.subtract, OP.mult), incs=[("cv", 1)])["cv"]

        final_dma = None
        for i in range(NFC):
            if i == 0:
                tap("mclip", mclipc[:, :], list(WQDONE))
            lm = []
            for j in range(4):
                w = []
                if j == 0:
                    w = [("cc2", MIXDONE)] + list(WQDONE)
                    if i > 0:
                        w.append(("cv", S.now("cv")))
                        w.append(("q_y", S.now("q_y")))
                lm.append(dma("sync", "q_f0", mixt[j][:, :],
                              cc2_out[i][j * 128:(j + 1) * 128, :], waits=w))
            ln_done = {}
            for j in range(4):
                wx = [("cv", S.now("cv"))] if (i > 0 or j > 0) else []
                xr = dma("sync", "q_f0", xres[0][:, :],
                         xmy[i * TC + j * 128: i * TC + (j + 1) * 128, :], waits=wx)
                v1 = E("vector", lambda e, j=j: e.tensor_tensor(
                    xln[j][:, :], mixt[j][:, :], xres[0][:, :], OP.add),
                    waits=[("q_f0", xr)], incs=[("cv", 1)])["cv"]
                v2 = E("vector", lambda e, j=j: e.tensor_tensor(
                    xln[j][:, :], xln[j][:, :], boutb[:, :], OP.add),
                    incs=[("cv", 1)])["cv"]
                if i == 0 and j == 0:
                    tap("mix0", mixt[0][:, :], [("q_f0", lm[0])])
                    tap("r0", xln[0][:, :], [("cv", v2)])
                ln_done[j] = layernorm(xln[j], j, v2)
                if i == 0 and j == 0:
                    tap("xln00", xln[0][:, :], [("cv", ln_done[0])])
                a1 = E("vector", lambda e, j=j: e.tensor_reduce(
                    col[:, 8:9], xln[j][:, :], AX.X, OP.max, apply_absolute_value=True),
                  waits=[("cs", S.now("cs"))], incs=[("cv", 1)])["cv"]
                a1b = E("scalar", lambda e: e.copy(col[:, 16:17], col[:, 8:9]),
                        waits=[("cv", a1)], incs=[("cs", 1)])["cs"]
                a3 = E("vector", lambda e: e.reciprocal(col[:, 9:10], col[:, 16:17]),
                       waits=[("cs", a1b)], incs=[("cv", 1)])["cv"]
                E("scalar", lambda e: e.activation(
                    col[:, 10:11], col[:, 9:10], AF.Identity, scale=127.0),
                  waits=[("cv", a3)], incs=[("cs", 1)])
                E("scalar", lambda e: e.activation(
                    col[:, 11:12], col[:, 16:17], AF.Identity, scale=1.0 / 127.0),
                  incs=[("cs", 1)])
                E("scalar", lambda e, j=j: e.activation(
                    dqg[:, j:j + 1], col[:, 11:12], AF.Identity,
                    scale=mclipc[:, 0:1]), incs=[("cs", 1)])
                a7s = E("scalar", lambda e, j=j: e.activation(
                    dqu[:, j:j + 1], col[:, 11:12], AF.Identity,
                    scale=mclipc[:, 1:2]), incs=[("cs", 1)])["cs"]
                a7 = a3
                if i == 0 and j == 0:
                    tap("qcol", col[:, :], [("cs", a7s)])
                for o in range(0, DM, TC):
                    E("scalar", lambda e, j=j, o=o: e.activation(
                        qint[:, :], xln[j][:, o:o + TC], AF.Identity,
                        scale=col[:, 10:11]),
                      waits=[("cv", a7)] + ([("q_xt", S.now("q_xt"))] if (j > 0 and o == 0) else []),
                      incs=[("cs", 1)])
                    E("scalar", lambda e, o=o: e.copy(xq_b[:, o:o + TC], qint[:, :]),
                      incs=[("cs", 1)])
                r2 = S.now("cs")
                for f in range(KM):
                    wt = [("cs", r2)]
                    if i > 0 and j == 0 and f == 0:
                        wt.append(("ct", S.now("ct")))
                    dmat("sync", "q_xt", xqT[f][:, j * 128:(j + 1) * 128],
                         xq_b[:, f * 128:(f + 1) * 128], waits=wt)
            xq_tr_done = S.now("q_xt")

            h_done = {}
            for n in range(KM):
                for k in range(KM):
                    w = [("ct", S.now("ct"))] if k == 0 else []
                    if i == 0 and n == 0 and k == 0:
                        w = w + list(WQDONE)
                    for q in range(4):
                        r_ = n * 4 + q
                        dma("gpsimd", "q_gw", wgl[k][:, q * 128:(q + 1) * 128],
                            wgT_d[(k * 32 + r_) * 128:(k * 32 + r_ + 1) * 128, :],
                            waits=(w if q == 0 else []))
                        dma("gpsimd", "q_gw", wul[k][:, q * 128:(q + 1) * 128],
                            wuT_d[(k * 32 + r_) * 128:(k * 32 + r_ + 1) * 128, :])
                wl_t = S.now("q_gw")
                if i == 0 and n == 0:
                    tap("wq0", wgl[0][:, :], [("q_gw", wl_t)], queue="gpsimd")
                    tap("xqT0", xqT[0][:, :], [("q_xt", xq_tr_done)], queue="gpsimd")
                for j in range(4):
                    for k in range(KM):
                        w = []
                        if k == 0:
                            w = [("q_gw", wl_t), ("q_xt", xq_tr_done),
                                 ("cv", S.now("cv"))]
                        E("tensor", lambda e, k=k, j=j: e.matmul(
                            p_op[0][:, :], xqT[k][:, j * 128:(j + 1) * 128],
                            wgl[k][:, :], start=(k == 0), stop=(k == KM - 1)),
                          waits=w, incs=([("ct", 1)] if k == KM - 1 else ()))
                    tg = S.now("ct")
                    for k in range(KM):
                        E("tensor", lambda e, k=k, j=j: e.matmul(
                            p_op[1][:, :], xqT[k][:, j * 128:(j + 1) * 128],
                            wul[k][:, :], start=(k == 0), stop=(k == KM - 1)),
                          incs=([("ct", 1)] if k == KM - 1 else ()))
                    tu = S.now("ct")
                    e1 = E("scalar", lambda e, j=j: e.activation(
                        relu2[:, :], p_op[0][:, :], AF.Silu, scale=dqg[:, j:j + 1]),
                        waits=[("ct", tg)], incs=[("cs", 1)])["cs"]
                    e2 = E("scalar", lambda e, j=j: e.activation(
                        abs2[:, :], p_op[1][:, :], AF.Identity, scale=dqu[:, j:j + 1]),
                        waits=[("ct", tu)], incs=[("cs", 1)])["cs"]
                    if i == 0 and n == 0 and j == 0:
                        tap("g00", relu2[:, :], [("cs", e1)])
                        tap("u00", abs2[:, :], [("cs", e2)])
                    h_done[(n, j)] = E("vector", lambda e, j=j, n=n: e.tensor_tensor(
                        hf[j][:, n * TC:(n + 1) * TC], relu2[:, :], abs2[:, :],
                        OP.mult), waits=[("cs", e2)], incs=[("cv", 1)])["cv"]

            if i == 0:
                tap("hf0", hf[0][:, 0:DM], [("cv", h_done[(1, 0)])], queue="gpsimd")
            hslot = [None, None]
            for j in range(4):
                w = [("cv", h_done[(KM - 1, j)]), ("cs", S.now("cs"))]
                b1 = E("vector", lambda e, j=j: e.tensor_reduce(
                    col[:, 12:13], hf[j][:, :], AX.X, OP.max, apply_absolute_value=True),
                  waits=w, incs=[("cv", 1)])["cv"]
                E("scalar", lambda e: e.activation(
                    col[:, 16:17], col[:, 12:13], AF.Relu,
                    bias=cc[:, C_NEPS:C_NEPS + 1]),
                  waits=[("cv", b1)], incs=[("cs", 1)])
                b2s = E("scalar", lambda e: e.activation(
                    col[:, 17:18], col[:, 16:17], AF.Identity,
                    bias=cc[:, C_EPS:C_EPS + 1]), incs=[("cs", 1)])["cs"]
                b3 = E("vector", lambda e: e.reciprocal(col[:, 13:14], col[:, 17:18]),
                       waits=[("cs", b2s)], incs=[("cv", 1)])["cv"]
                E("scalar", lambda e: e.activation(
                    col[:, 14:15], col[:, 13:14], AF.Identity, scale=127.0),
                  waits=[("cv", b3)], incs=[("cs", 1)])
                E("scalar", lambda e: e.activation(
                    col[:, 15:16], col[:, 17:18], AF.Identity, scale=1.0 / 127.0),
                  incs=[("cs", 1)])
                E("scalar", lambda e, j=j: e.activation(
                    dqd[:, j:j + 1], col[:, 15:16], AF.Identity,
                    scale=mclipc[:, 2:3]), incs=[("cs", 1)])
                b6 = b3
                for pi, o in enumerate(range(0, INNER, TC)):
                    buf = pi % 2
                    wq_ = [("cv", b6)]
                    if hslot[buf] is not None:
                        wq_.append((f"q_h{buf}", hslot[buf]))
                    E("scalar", lambda e, j=j, o=o: e.activation(
                        qint[:, :], hf[j][:, o:o + TC], AF.Identity,
                        scale=col[:, 14:15]), waits=wq_, incs=[("cs", 1)])
                    r2 = E("scalar", lambda e, buf=buf: e.copy(hq_b[buf][:, :],
                                                               qint[:, :]),
                           incs=[("cs", 1)])["cs"]
                    for fo in range(4):
                        f = o // 128 + fo
                        wt = [("cs", r2)]
                        if i > 0 and j == 0 and f == 0:
                            wt.append(("ct", S.now("ct")))
                        hslot[buf] = dmat("sync", f"q_h{buf}",
                                          hqT[f][:, j * 128:(j + 1) * 128],
                                          hq_b[buf][:, fo * 128:(fo + 1) * 128],
                                          waits=wt)
            hq_done = [("q_h0", S.now("q_h0")), ("q_h1", S.now("q_h1"))]

            p_dn = [p_op[0], p_op[1], p_mm[0], p_mm[1]]
            down_done = {}
            for n in range(2):
                for kh in range(2):
                    for kk in range(16):
                        w = [("ct", S.now("ct"))] if kk == 0 else []
                        if i == 0 and n == 0 and kh == 0 and kk == 0:
                            w = w + list(WQDONE)
                        for q in range(4):
                            r_ = n * 4 + q
                            kfull = kh * 16 + kk
                            dma("gpsimd", "q_gw", wdl[kk][:, q * 128:(q + 1) * 128],
                                wdT_d[(kfull * 8 + r_) * 128:(kfull * 8 + r_ + 1) * 128, :],
                                waits=(w if q == 0 else []))
                    wl_t = S.now("q_gw")
                    for j in range(4):
                        for kk in range(16):
                            k = kh * 16 + kk
                            w = []
                            if kk == 0:
                                w = [("q_gw", wl_t), ("cv", S.now("cv"))] + list(hq_done)
                            E("tensor", lambda e, kk=kk, k=k, j=j, n=n: e.matmul(
                                p_dn[j][:, :], hqT[k][:, j * 128:(j + 1) * 128],
                                wdl[kk][:, :], start=(k == 0), stop=(k == KI - 1)),
                              waits=w, incs=([("ct", 1)] if kk == 15 else ()))
                    if kh == 1:
                        for j in range(4):
                            td = S.now("ct") - (3 - j)
                            down_done[(j, n)] = E(
                                "vector", lambda e, j=j, n=n: e.scalar_tensor_tensor(
                                    outt[j][:, n * TC:(n + 1) * TC], p_dn[j][:, :],
                                    dqd[:, j:j + 1], xln[j][:, n * TC:(n + 1) * TC],
                                    OP.mult, OP.add),
                                waits=[("ct", td)], incs=[("cv", 1)])["cv"]

            for j in range(4):
                lnd = layernorm(outt[j], j, down_done[(j, 1)])
                final_dma = dma("sync", "q_y",
                                y_out[i * TC + j * 128: i * TC + (j + 1) * 128, :],
                                outt[j][:, :], waits=[("cv", lnd)])

        E("sync", None, waits=[("q_y", S.now("q_y")), ("q_tap", S.now("q_tap"))])

        semh = {k: sems[v] for k, v in SEMMAP.items()}
        with nc.Block() as block:
            def make(engname):
                def runner(eng):
                    for fn, waits, incs in S.ops[engname]:
                        for s, v in waits:
                            eng.wait_ge(semh[s], v)
                        if fn is None:
                            continue
                        ins = fn(eng)
                        for s, d in incs:
                            ins = ins.then_inc(semh[s], d)
                return runner
            block.sync(make("sync"))
            block.scalar(make("scalar"))
            block.vector(make("vector"))
            block.tensor(make("tensor"))
            block.gpsimd(make("gpsimd"))
        ffn.close()

    return nc


def make_inputs(inp):
    import ml_dtypes
    x = np.ascontiguousarray(np.asarray(inp["L0_attn_x"], dtype=np.float32))
    win_full = np.asarray(inp["L0_attn_in_proj_w"], dtype=np.float32)
    b_in_full = np.asarray(inp["L0_attn_in_proj_b"], dtype=np.float32)
    convw_f = np.asarray(inp["L0_attn_conv_w"], dtype=np.float32).reshape(DI, 4)
    convb_f = np.asarray(inp["L0_attn_conv_b"], dtype=np.float32)
    wxp_full = np.asarray(inp["L0_attn_x_proj_w"], dtype=np.float32)
    bxp_full = np.asarray(inp["L0_attn_x_proj_b"], dtype=np.float32)
    dtw_full = np.asarray(inp["L0_attn_dt_proj_w"], dtype=np.float32)
    dtb_full = np.asarray(inp["L0_attn_dt_proj_b"], dtype=np.float32)
    dvec_full = np.asarray(inp["L0_attn_D"], dtype=np.float32)
    wout_full = np.asarray(inp["L0_attn_out_proj_w"], dtype=np.float32)
    bout_full = np.asarray(inp["L0_attn_out_proj_b"], dtype=np.float32)
    wg_f = np.ascontiguousarray(np.asarray(inp["L0_ffn_w_gate"], dtype=np.float32))
    wu_f = np.ascontiguousarray(np.asarray(inp["L0_ffn_w_up"], dtype=np.float32))
    wd_f = np.ascontiguousarray(np.asarray(inp["L0_ffn_w_down"], dtype=np.float32))

    consts = np.zeros((128, NCONST), np.float32)
    consts[:, C_ONE] = 1.0
    for s in range(16):
        consts[:, C_NEGS + s] = -(s + 1)
    consts[:, C_EPS] = 1e-5
    consts[:, C_NEPS] = -1e-5
    onesbf = np.ones((1, 128), ml_dtypes.bfloat16)
    selbf = np.zeros((16, 16 * 128), ml_dtypes.bfloat16)
    for s_ in range(16):
        selbf[s_, s_ * 128:(s_ + 1) * 128] = 1
    onesf = np.ones((128, 1), np.float32)
    onesfr = np.ones((1, 128), np.float32)
    ident = np.eye(128).astype(ml_dtypes.bfloat16)

    wxp_cols = np.concatenate([
        wxp_full[:, 0:32], wxp_full[:, 32:32 + DH],
        wxp_full[:, 0:32], wxp_full[:, 32 + DH:32 + DI]], axis=1)

    maps = []
    for core in range(8):
        b, p = core >> 1, core & 1
        dh = slice(p * DH, (p + 1) * DH)
        win_my = np.concatenate(
            [win_full[:, dh], win_full[:, DI + p * DH: DI + (p + 1) * DH]], axis=1)
        b_in_my = np.concatenate(
            [b_in_full[dh], b_in_full[DI + p * DH: DI + (p + 1) * DH]])
        maps.append({
            "xfull": np.ascontiguousarray(x[b]),
            "xmy": np.ascontiguousarray(x[b, p * TOK:(p + 1) * TOK]),
            "win": np.ascontiguousarray(win_my),
            "b_in": np.ascontiguousarray(b_in_my),
            "wxp": np.ascontiguousarray(wxp_cols[dh, :]),
            "bxp_u": np.ascontiguousarray(bxp_full[32 + p * DH: 32 + (p + 1) * DH]),
            "bxp_ds": np.ascontiguousarray(bxp_full[0:16]),
            "bxp_b": np.ascontiguousarray(bxp_full[16:32]),
            "dtw": np.ascontiguousarray(dtw_full[:, dh]),
            "dtb": np.ascontiguousarray(dtb_full[dh]),
            "convw": np.ascontiguousarray(convw_f[dh, :]),
            "convb": np.ascontiguousarray(convb_f[dh]),
            "dvec": np.ascontiguousarray(dvec_full[dh]),
            "wout": np.ascontiguousarray(wout_full[dh, :]),
            "bout": np.ascontiguousarray(bout_full),
            "wg": wg_f, "wu": wu_f, "wd": wd_f,
            "consts": consts, "onesbf": onesbf, "selbf": selbf, "onesf": onesf,
            "onesfr": onesfr, "ident": ident,
        })
    return maps


_NC_CACHE = {}


def kernel(**inputs):
    if "main" not in _NC_CACHE:
        _NC_CACHE["main"] = build_nc()
    nc = _NC_CACHE["main"]
    maps = make_inputs(inputs)
    res = run_bass_kernel_spmd(nc, maps, core_ids=list(range(8)))
    out = np.empty((B, T, DM), np.float32)
    for core in range(8):
        b, p = core >> 1, core & 1
        out[b, p * TOK:(p + 1) * TOK, :] = res.results[core]["y"]
    return out
